# version 14 (final)
# baseline (speedup 1.0000x reference)
import math
import numpy as np
import ml_dtypes
import concourse.bass as bass
import concourse.mybir as mybir
from concourse.ap import AP
from concourse.bass_utils import run_bass_kernel_spmd
from contextlib import ExitStack

F32 = mybir.dt.float32
BF16 = mybir.dt.bfloat16
U32 = mybir.dt.uint32
U8 = mybir.dt.uint8
AF = mybir.ActivationFunctionType
ALU = mybir.AluOpType
BF = ml_dtypes.bfloat16
SP = mybir.EngineType.SP

NCORES = 8


def make_cfg(full=True):
    c = {}
    if full:
        c.update(B=2, S=8192, E=2048, H=16, D=128, INTER=8192, COMPRESSION=128.9)
    else:
        c.update(B=2, S=2048, E=512, H=4, D=128, INTER=1024, COMPRESSION=128.9)
    c["TOK"] = c["B"] * c["S"] // NCORES
    c["TCH"] = min(512, c["TOK"] // 2)
    c["NT"] = c["TOK"] // c["TCH"]
    c["F"] = c["E"] // 128
    c["NHG"] = 4 * c["H"] // 16
    assert c["NHG"] * 8 == c["B"] * c["H"]
    c["QB"] = c["S"] // 8
    c["NQB"] = 8
    c["QT"] = c["QB"] // 128
    c["KC"] = min(512, c["QB"])
    c["NKC"] = c["S"] // c["KC"]
    c["GROWS"] = c["H"] * 3 * c["D"] + 2 * c["H"]
    c["NORM0"] = c["H"] * 3 * c["D"]
    c["GOROWS"] = c["NHG"] * c["D"]
    c["NSAMP"] = c["E"] * c["INTER"]
    c["NCTRL"] = max(16, int(c["NSAMP"] / c["COMPRESSION"]))
    c["R"] = c["NSAMP"] // 128
    c["RT"] = c["R"] // 128
    c["ITILES"] = c["INTER"] // 128
    c["SM_SCALE"] = 1.0 / math.sqrt(c["D"])
    return c



def spline_grid(C):
    n = C["NSAMP"]
    nct = C["NCTRL"]
    t = np.linspace(0.0, nct - 1.0, n, dtype=np.float64)
    i = np.clip(np.floor(t).astype(np.int64), 0, nct - 2)
    R = C["R"]
    RT = C["RT"]
    r = np.arange(R)
    I_r = i[r * 128]
    i_rows = i.reshape(R, 128)
    assert (i_rows.max(axis=1) - I_r <= 1).all() and (i_rows.min(axis=1) == I_r).all()
    phi = (t[r * 128] - I_r)
    base_T = I_r.reshape(RT, 128)[:, 0]
    Ir2 = I_r.reshape(RT, 128)
    d = (base_T[:, None] + np.arange(128)[None, :]) - Ir2
    assert d.min() >= 0 and d.max() <= 1, (d.min(), d.max())
    dmask = d.T.astype(np.float32).copy()
    phi_a = phi.reshape(RT, 128).T.astype(np.float32).copy()
    phi_b = (phi - 1.0).reshape(RT, 128).T.astype(np.float32).copy()
    scount = (i_rows == I_r[:, None]).sum(axis=1)
    dt = (nct - 1.0) / (n - 1.0)
    return dict(I_r=I_r, base_T=base_T, dmask=dmask, phi_a=phi_a, phi_b=phi_b,
                scount=scount, dt=dt)


def spline_masks(C, g):
    R = C["R"]
    sc = g["scount"]
    s = np.arange(128)
    mrow = (s[None, :] >= sc[:, None])
    F = C["F"]
    mg = mrow.reshape(C["INTER"], F, 128).transpose(2, 1, 0)
    IT = C["ITILES"]
    md = mrow.reshape(C["E"], IT, 128).transpose(2, 1, 0)
    return (np.ascontiguousarray(mg.astype(np.uint8)),
            np.ascontiguousarray(md.astype(np.uint8)))


def spline_vsel(C, g):
    dt = g["dt"]
    s = np.arange(128, dtype=np.float64)
    pw = np.stack([(s * dt) ** m for m in range(4)])
    v = np.zeros((128, 8 * 128), dtype=np.float64)
    for gg in range(4):
        for hp in range(4):
            for ev in range(2):
                for m in range(4):
                    v[32 * gg + 8 * hp + 4 * ev + m,
                      (hp * 2 + ev) * 128:(hp * 2 + ev + 1) * 128] = pw[m]
    return np.ascontiguousarray(v.astype(BF))


def spline_eperm(C):
    F = C["F"]
    K1 = 128 // F
    Eg = np.zeros((K1, 8, 128, 128), dtype=np.float64)
    for il in range(K1):
        for m in range(8):
            for h16 in range(F):
                Eg[il, m, F * il + h16, 8 * h16 + m] = 1.0
    IT = C["ITILES"]
    hh = 128 // IT
    NQ = max(1, IT // 16)
    Ed = np.zeros((hh, NQ, 8, 128, 128), dtype=np.float64)
    for eps in range(hh):
        for q in range(NQ):
            for m in range(8):
                for ip in range(min(16, IT)):
                    i16 = 16 * q + ip
                    if i16 < IT:
                        Ed[eps, q, m, IT * eps + i16, 8 * ip + m] = 1.0
    return (np.ascontiguousarray(Eg.astype(BF)),
            np.ascontiguousarray(Ed.astype(BF)))


def mload_runs(C, g):
    base = g["base_T"]
    RT = C["RT"]
    runs = []
    t0 = 0
    while t0 < RT:
        t1 = t0 + 1
        step = None
        while t1 < RT and (t1 % 128 != 0):
            st = int(base[t1] - base[t1 - 1])
            if step is None:
                step = st
            elif st != step:
                break
            t1 += 1
        runs.append((t0, t1 - t0, int(step if step is not None else 0)))
        t0 = t1
    return runs


def rope_tables(C, positions):
    D = C["D"]
    inv = 1.0 / (10000.0 ** (np.arange(0, D, 2, dtype=np.float64) / D))
    j = np.arange(D)
    invj = inv[j % (D // 2)]
    ang = positions[None, :].astype(np.float64) * invj[:, None]
    return np.cos(ang).astype(np.float32), np.sin(ang).astype(np.float32)


def rot_matrix():
    Rm = np.zeros((128, 128), dtype=np.float64)
    for i in range(64):
        Rm[2 * i, 2 * i + 1] = -1.0
        Rm[2 * i + 1, 2 * i] = 1.0
    return np.ascontiguousarray(Rm.T.astype(BF))


def diag_masks(C):
    m = np.zeros((128, 4 * 512), dtype=np.float32)
    for o in range(4):
        kk = np.arange(128)[:, None]
        qc = np.arange(512)[None, :]
        m[:, o * 512:(o + 1) * 512] = np.where(o * 128 + kk > qc, -1e30, 0.0)
    return m.astype(BF)



class Ctr:
    def __init__(self, es, nc, name):
        self.h = es.enter_context(nc.semaphore(name))
        self.n = 0

    def inc(self, instr, k=1):
        instr.then_inc(self.h, k)
        self.n += k

    def dinc(self, instr):
        instr.then_inc(self.h, 16)
        self.n += 16


def sbap(handle, rowlen, p0, coff, dims):
    pcount = dims[0]
    free = dims[1:]
    return AP(handle, p0 * rowlen + coff, [[rowlen, pcount]] + [list(x) for x in free])


def build_program(C, HOST):
    grid = HOST["grid"]
    nc = bass.Bass("TRN2", target_bir_lowering=False)
    E, F, TOK, TCH, NT = C["E"], C["F"], C["TOK"], C["TCH"], C["NT"]
    H, D, S, QB, KC, QT = C["H"], C["D"], C["S"], C["QB"], C["KC"], C["QT"]
    NHG, GROWS, NORM0, GOROWS = C["NHG"], C["GROWS"], C["NORM0"], C["GOROWS"]
    INTER, ITILES, R, RT, NCTRL = C["INTER"], C["ITILES"], C["R"], C["RT"], C["NCTRL"]
    SUB = min(512, TCH)
    NSUB = TCH // SUB
    K1 = 128 // F
    hh = 128 // ITILES if ITILES < 128 else 1
    hh = max(1, 128 // ITILES)
    NQ = max(1, ITILES // 16)
    NOFF = KC // 128

    dram = lambda n, sh, dt: nc.dram_tensor(n, sh, dt)

    xin = nc.dram_tensor("xin", [E, TOK], F32, kind="ExternalInput")
    wqkv = nc.dram_tensor("wqkv", [3 * E, E], F32, kind="ExternalInput")
    wout = nc.dram_tensor("wout", [E, E], F32, kind="ExternalInput")
    wq = nc.dram_tensor("wq", [E, E], F32, kind="ExternalInput")
    wk = nc.dram_tensor("wk", [E, E], F32, kind="ExternalInput")
    wv = nc.dram_tensor("wv", [E, E], F32, kind="ExternalInput")
    wo = nc.dram_tensor("wo", [E, E], F32, kind="ExternalInput")
    biases = nc.dram_tensor("biases", [128, 3 * F], F32, kind="ExternalInput")
    cpads = nc.dram_tensor("cpads", [3, NCTRL + 8], F32, kind="ExternalInput")
    maskg = nc.dram_tensor("maskg", [128, F * INTER], U8, kind="ExternalInput")
    maskd = nc.dram_tensor("maskd", [128, ITILES * E], U8, kind="ExternalInput")
    vseli = nc.dram_tensor("vseli", [128, 1024], BF16, kind="ExternalInput")
    egi = nc.dram_tensor("egi", [K1 * 8 * 128, 128], BF16, kind="ExternalInput")
    edi = nc.dram_tensor("edi", [hh * NQ * 8 * 128, 128], BF16, kind="ExternalInput")
    identfi = nc.dram_tensor("identfi", [128, 128], F32, kind="ExternalInput")
    identbi = nc.dram_tensor("identbi", [128, 128], BF16, kind="ExternalInput")
    onesi = nc.dram_tensor("onesi", [128, 132], F32, kind="ExternalInput")
    rmti = nc.dram_tensor("rmti", [128, 128], BF16, kind="ExternalInput")
    cosi = nc.dram_tensor("cosi", [128, TOK], F32, kind="ExternalInput")
    sini = nc.dram_tensor("sini", [128, TOK], F32, kind="ExternalInput")
    dmti = nc.dram_tensor("dmti", [128, 4 * 512], BF16, kind="ExternalInput")
    coffi = nc.dram_tensor("coffi", [1, 4], U32, kind="ExternalInput")
    phii = nc.dram_tensor("phii", [128, 2 * RT], F32, kind="ExternalInput")
    dmaski = nc.dram_tensor("dmaski", [128, RT], U8, kind="ExternalInput")
    xout = nc.dram_tensor("xout", [E, TOK], F32, kind="ExternalOutput")
    import os
    KDBG = bool(os.environ.get("KDBG"))
    if KDBG:
        dbg1 = nc.dram_tensor("dbg1", [E, TOK], F32, kind="ExternalOutput")
        dbg2 = nc.dram_tensor("dbg2", [E, TOK], F32, kind="ExternalOutput")
        dbg3 = nc.dram_tensor("dbg3", [E, TOK], F32, kind="ExternalOutput")

    x1 = dram("x1", [E, TOK], F32)
    x2 = dram("x2", [E, TOK], F32)
    x3 = dram("x3", [E, TOK], F32)
    wqkvT = dram("wqkvT", [F, 128, 3 * E], BF16)
    woutT = dram("woutT", [F, 128, E], BF16)
    gwT = dram("gwT", [F, 128, INTER], BF16)
    uwT = dram("uwT", [F, 128, INTER], BF16)
    dwT = dram("dwT", [ITILES, 128, E], BF16)
    gin = dram("gin", [NT * GROWS, TCH], BF16)
    G = nc.dram_tensor("G", [NT * 8 * GROWS, TCH], BF16, addr_space="Shared")
    goin = dram("goin", [GOROWS, S], BF16)
    GO = nc.dram_tensor("GO", [NHG * 8 * D, S], BF16, addr_space="Shared")
    GL = dram("GL", [4 * NHG * 3 * D, TOK], BF16)
    NL = dram("NL", [8 * NHG, TOK], BF16)
    AOL = dram("AOL", [E, TOK], BF16)

    es = ExitStack()
    _uid = [0]

    def _nm(n):
        _uid[0] += 1
        return f"{n}_{_uid[0]}"

    ident_f = es.enter_context(nc.sbuf_tensor(_nm("ident_f"), [128, 128], F32))
    ident_b = es.enter_context(nc.sbuf_tensor(_nm("ident_b"), [128, 128], BF16))
    ones_f = es.enter_context(nc.sbuf_tensor(_nm("ones_f"), [128, 132], F32))
    nc.const_aps.register(F32, 1e-5, ones_f[:, 130:131])
    nc.const_aps.register(F32, 1.0, ones_f[:, 0:1])
    nc.const_aps.register(F32, 0.0, ones_f[:, 131:132])
    ones_b = es.enter_context(nc.sbuf_tensor(_nm("ones_b"), [128, 2], BF16))
    rmt_sb = es.enter_context(nc.sbuf_tensor(_nm("rmt_sb"), [128, 128], BF16))
    cos_sb = es.enter_context(nc.sbuf_tensor(_nm("cos_sb"), [128, TOK], F32))
    sin_sb = es.enter_context(nc.sbuf_tensor(_nm("sin_sb"), [128, TOK], F32))
    dmt_sb = es.enter_context(nc.sbuf_tensor(_nm("dmt_sb"), [128, 4 * 512], BF16))
    bias_sb = es.enter_context(nc.sbuf_tensor(_nm("bias_sb"), [128, 3 * F], F32))
    coff_sb = es.enter_context(nc.sbuf_tensor(_nm("coff_sb"), [1, 4], U32))
    sumx_sb = es.enter_context(nc.sbuf_tensor(_nm("sumx_sb"), [1, TOK], F32))
    sumx2_sb = es.enter_context(nc.sbuf_tensor(_nm("sumx2_sb"), [1, TOK], F32))

    sems = {}
    for name in ["sd", "w0", "w1", "x0", "x1", "s0", "s1", "r0", "r1",
                 "o0", "o1", "g0", "g1", "vd", "kd", "ad", "gd", "cc", "pe",
                 "pw", "po", "se", "ve", "vf", "fi", "fo", "so", "sp", "od"]:
        sems[name] = Ctr(es, nc, name)
    sd, kd, ad, gd, cc = (sems[k] for k in ["sd", "kd", "ad", "gd", "cc"])
    pe, pw, se, ve, vf, sp, od = (sems[k] for k in ["pe", "pw", "se", "ve", "vf", "sp", "od"])

    regs = {}

    with nc.Block() as blk:
        @blk.sync
        def _(sy):
            for dst, src in [(ident_f, identfi), (ident_b, identbi), (ones_f, onesi),
                             (rmt_sb, rmti), (cos_sb, cosi), (sin_sb, sini),
                             (dmt_sb, dmti), (bias_sb, biases), (coff_sb, coffi)]:
                sd.dinc(sy.dma_start(out=dst[:, :], in_=src.ap()[:, :]))
            sy.wait_ge(sd.h, sd.n)
            regs["g"] = nc.values_load(coff_sb[0:1, 0:1], engines=[SP],
                                       min_val=0, max_val=8 * GROWS * TOK,
                                       skip_runtime_bounds_check=True)
            regs["nq"] = nc.values_load(coff_sb[0:1, 1:2], engines=[SP],
                                        min_val=0, max_val=8 * GROWS * TOK,
                                        skip_runtime_bounds_check=True)
            regs["go"] = nc.values_load(coff_sb[0:1, 2:3], engines=[SP],
                                        min_val=0, max_val=8 * GOROWS * S,
                                        skip_runtime_bounds_check=True)

        @blk.gpsimd
        def _(g):
            g.wait_ge(sd.h, sd.n)
            gd.dinc(g.dma_start(out=ones_b[:, :], in_=onesi.ap()[:, 0:2]))
            g.wait_ge(gd.h, gd.n)

    def ln_rows_s1(sc):
        sc.mul(mrow_sb[0:1, :], sumx_sb[0:1, :], 1.0 / E)
        sc.drain()
        sc.square(trow_sb[0:1, :], mrow_sb[0:1, :])
        se.inc(sc.mul(rrow_sb[0:1, :], sumx2_sb[0:1, :], 1.0 / E))

    def ln_rows_v(v):
        ve.inc(v.tensor_tensor(trow_sb[0:1, :], rrow_sb[0:1, :], trow_sb[0:1, :],
                               ALU.subtract))

    def ln_rows_s2(sc):
        sc.activation(trow_sb[0:1, :], trow_sb[0:1, :], AF.Ln, bias=1e-5)
        sc.drain()
        se.inc(sc.activation(rrow_sb[0:1, :], trow_sb[0:1, :], AF.Exp, scale=-0.5))

    SS = 512 if TOK >= 512 else TOK
    NS = TOK // SS

    with nc.Block() as blk:
        p0s = ExitStack()
        xt = p0s.enter_context(nc.sbuf_tensor(_nm("p0_xt"), [128, TOK], F32))
        xb = p0s.enter_context(nc.sbuf_tensor(_nm("p0_xb"), [128, TOK], BF16))
        x2b = p0s.enter_context(nc.sbuf_tensor(_nm("p0_x2b"), [128, TOK], BF16))
        psx = [p0s.enter_context(nc.psum_tensor(_nm(f"p0_psx{i}"), [1, SS], F32))
               for i in range(NS)]
        psx2 = [p0s.enter_context(nc.psum_tensor(_nm(f"p0_psx2{i}"), [1, SS], F32))
                for i in range(NS)]
        sd0, se0, pe0 = sd.n, se.n, pe.n

        @blk.sync
        def _(sy):
            for f in range(F):
                if f >= 1:
                    sy.wait_ge(se.h, se0 + 2 * f)
                sd.dinc(sy.dma_start(out=xt[:, :], in_=xin.ap()[f * 128:(f + 1) * 128, :]))

        @blk.scalar
        def _(sc):
            for f in range(F):
                sc.wait_ge(sd.h, sd0 + 16 * (f + 1))
                if f >= 1:
                    sc.wait_ge(pe.h, pe0 + f)
                se.inc(sc.copy(xb[:, :], xt[:, :]))
                se.inc(sc.square(x2b[:, :], xt[:, :]))

        @blk.tensor
        def _(t):
            for f in range(F):
                t.wait_ge(se.h, se0 + 2 * (f + 1))
                for s_ in range(NS):
                    t.matmul(psx[s_][0:1, :], ones_b[:, 0:1], xb[:, s_ * SS:(s_ + 1) * SS],
                             start=(f == 0), stop=(f == F - 1))
                    last = t.matmul(psx2[s_][0:1, :], ones_b[:, 0:1],
                                    x2b[:, s_ * SS:(s_ + 1) * SS],
                                    start=(f == 0), stop=(f == F - 1))
                pe.inc(last)

        @blk.vector
        def _(v):
            v.wait_ge(pe.h, pe0 + F)
            for s_ in range(NS):
                v.tensor_copy(sumx_sb[0:1, s_ * SS:(s_ + 1) * SS], psx[s_][0:1, :])
                last = v.tensor_copy(sumx2_sb[0:1, s_ * SS:(s_ + 1) * SS], psx2[s_][0:1, :])
            ve.inc(last)
        p0s.close()

    for wname, wsrc, wdst, OC in [("wqkv", wqkv, wqkvT, 3 * E // 128),
                                  ("wout", wout, woutT, E // 128)]:
        with nc.Block() as blk:
            ps_ = ExitStack()
            wtmp = [ps_.enter_context(nc.sbuf_tensor(_nm(f"{wname}_t{i}"), [128, 128], F32))
                    for i in range(2)]
            wbf = [ps_.enter_context(nc.sbuf_tensor(_nm(f"{wname}_b{i}"), [128, 128], BF16))
                   for i in range(2)]
            pst = [ps_.enter_context(nc.psum_tensor(_nm(f"{wname}_p{i}"), [128, 128], BF16))
                   for i in range(2)]
            rowb = ps_.enter_context(nc.sbuf_tensor(_nm(f"{wname}_row"), [128, OC * 128], BF16))
            se0, pe0, ve0, ad0 = se.n, pe.n, ve.n, ad.n
            NTL = F * OC
            wsem = [sems["w0"], sems["w1"]]
            wval = {}

            def idx(f, oc):
                return f * OC + oc

            @blk.sync
            def _(sy):
                for f in range(F):
                    for oc in range(OC):
                        i = idx(f, oc)
                        if i >= 2:
                            sy.wait_ge(ve.h, ve0 + i - 1)
                        src = AP(wsrc, (oc * 128) * E + f * 128, [[E, 128], [1, 128]])
                        wsem[i % 2].dinc(sy.dma_start(out=wtmp[i % 2][:, :], in_=src))
                        wval[i] = wsem[i % 2].n

            @blk.vector
            def _(v):
                for f in range(F):
                    for oc in range(OC):
                        i = idx(f, oc)
                        v.wait_ge(wsem[i % 2].h, wval[i])
                        if i >= 2:
                            v.wait_ge(pe.h, pe0 + i - 1)
                        ve.inc(v.tensor_copy(wbf[i % 2][:, :], wtmp[i % 2][:, :]))

            @blk.tensor
            def _(t):
                for f in range(F):
                    for oc in range(OC):
                        i = idx(f, oc)
                        t.wait_ge(ve.h, ve0 + i + 1)
                        if i >= 2:
                            t.wait_ge(se.h, se0 + i - 1)
                        pe.inc(t.transpose(pst[i % 2][:, :], wbf[i % 2][:, :], ident_b[:, :]))

            @blk.scalar
            def _(sc):
                for f in range(F):
                    sc.wait_ge(ad.h, ad0 + 16 * f)
                    for oc in range(OC):
                        i = idx(f, oc)
                        sc.wait_ge(pe.h, pe0 + i + 1)
                        se.inc(sc.copy(rowb[:, oc * 128:(oc + 1) * 128], pst[i % 2][:, :]))

            @blk.gpsimd
            def _(g):
                for f in range(F):
                    g.wait_ge(se.h, se0 + (f + 1) * OC)
                    ad.dinc(g.dma_start(out=AP(wdst, f * 128 * OC * 128,
                                               [[OC * 128, 128], [1, OC * 128]]),
                                        in_=rowb[:, :]))
                g.wait_ge(ad.h, ad.n)
            ps_.close()

    MW = 136
    NCH = RT // 128 if RT >= 128 else 1
    CHT = min(128, RT)
    RH = min(512, RT)
    NRH = RT // RH

    def p0d(widx, runs):
        ssp = ExitStack()
        pk = ssp.enter_context(nc.sbuf_tensor(_nm("sp_pk"), [128, 5 * RT], F32))
        ca = ssp.enter_context(nc.sbuf_tensor(_nm("sp_ca"), [128, 8 * RT], F32))
        psz = ssp.enter_context(nc.psum_tensor(_nm("sp_psz"), [128, RH], F32))

        def P(k):
            return pk[:, k * RT:(k + 1) * RT]

        def CA(m):
            return ca[:, m * RT:(m + 1) * RT]

        inner = ExitStack()
        m_sb = [inner.enter_context(nc.sbuf_tensor(_nm("sp_m"), [CHT, MW], F32))
                for i in range(2)]
        w7 = inner.enter_context(nc.sbuf_tensor(_nm("sp_w7"), [128, 6 * RT], F32))
        psm = [inner.enter_context(nc.psum_tensor(_nm("sp_psm"), [128, 128], F32))
               for i in range(2)]
        se0, pe0, ve0 = se.n, pe.n, ve.n
        msem = [sems["x0"], sems["x1"]]
        chval = {}
        with nc.Block() as blk:
            @blk.sync
            def _(sy):
                for ch in range(NCH):
                    if ch >= 2:
                        sy.wait_ge(pe.h, pe0 + 6 * (ch - 1))
                    for (t0, ln, step) in runs:
                        if not (ch * 128 <= t0 < (ch + 1) * 128):
                            continue
                        src = AP(cpads, widx * (NCTRL + 8) + int(grid["base_T"][t0]),
                                 [[step, ln], [1, MW - 2]])
                        dst = m_sb[ch % 2][t0 - ch * 128: t0 - ch * 128 + ln, 0:MW - 2]
                        msem[ch % 2].dinc(sy.dma_start(out=dst, in_=src))
                    chval[ch] = msem[ch % 2].n

            @blk.tensor
            def _(t):
                for ch in range(NCH):
                    t.wait_ge(msem[ch % 2].h, chval[ch])
                    for c in range(6):
                        j = 6 * ch + c
                        if j >= 2:
                            t.wait_ge(se.h, se0 + j - 1)
                        pe.inc(t.transpose(psm[c % 2][:, 0:CHT],
                                           m_sb[ch % 2][:, c:c + 128],
                                           ident_f[0:CHT, 0:CHT]))

            @blk.scalar
            def _(sc):
                for ch in range(NCH):
                    for c in range(6):
                        sc.wait_ge(pe.h, pe0 + ch * 6 + c + 1)
                        se.inc(sc.copy(w7[:, c * RT + ch * CHT: c * RT + ch * CHT + CHT],
                                       psm[c % 2][:, 0:CHT]))

            @blk.vector
            def _(v):
                v.wait_ge(se.h, se0 + 6 * NCH)
                for k in range(5):
                    v.tensor_copy(P(k), w7[:, (k + 1) * RT:(k + 2) * RT])
                    v.drain()
                    ve.inc(v.copy_predicated(P(k), dmask_sb[:, :],
                                             w7[:, k * RT:(k + 1) * RT]))
                    v.drain()
        inner.close()

        inner = ExitStack()
        tmp = inner.enter_context(nc.sbuf_tensor(_nm("sp_tmp"), [128, 5 * RT], F32))
        with nc.Block() as blk:
            @blk.vector
            def _(v):
                def T_(i):
                    return tmp[:, i * RT:(i + 1) * RT]

                def vop(op):
                    v.drain()
                for e in range(2):
                    phi = php_sb[:, e * RT:(e + 1) * RT]
                    p0_, p1_, p2_, p3_ = P(e), P(e + 1), P(e + 2), P(e + 3)
                    A1, A2, A3, u1, u2 = T_(0), T_(1), T_(2), T_(3), T_(4)
                    vop(v.tensor_tensor(u1, p2_, p0_, ALU.subtract))
                    vop(v.tensor_scalar(A1, u1, 0.5, None, ALU.mult))
                    vop(v.scalar_tensor_tensor(u2, p1_, -2.5, p0_, ALU.mult, ALU.add))
                    vop(v.scalar_tensor_tensor(u1, p3_, -0.5, u2, ALU.mult, ALU.add))
                    vop(v.scalar_tensor_tensor(A2, p2_, 2.0, u1, ALU.mult, ALU.add))
                    vop(v.tensor_tensor(u1, p3_, p0_, ALU.subtract))
                    vop(v.tensor_tensor(u2, p1_, p2_, ALU.subtract))
                    vop(v.tensor_scalar(u1, u1, 0.5, None, ALU.mult))
                    vop(v.scalar_tensor_tensor(A3, u2, 1.5, u1, ALU.mult, ALU.add))
                    vop(v.tensor_copy(CA(e * 4 + 3), A3))
                    vop(v.tensor_tensor(u1, A3, phi, ALU.mult))
                    vop(v.scalar_tensor_tensor(CA(e * 4 + 2), u1, 3.0, A2,
                                               ALU.mult, ALU.add))
                    vop(v.tensor_tensor(u2, A2, CA(e * 4 + 2), ALU.add))
                    vop(v.tensor_tensor(u1, u2, phi, ALU.mult))
                    vop(v.tensor_tensor(CA(e * 4 + 1), u1, A1, ALU.add))
                    vop(v.tensor_tensor(u1, A3, phi, ALU.mult))
                    vop(v.tensor_tensor(u2, u1, A2, ALU.add))
                    vop(v.tensor_tensor(u1, u2, phi, ALU.mult))
                    vop(v.tensor_tensor(u2, u1, A1, ALU.add))
                    vop(v.tensor_tensor(u1, u2, phi, ALU.mult))
                    vop(v.tensor_tensor(CA(e * 4 + 0), u1, p1_, ALU.add))
        inner.close()

        inner = ExitStack()
        cab = inner.enter_context(nc.sbuf_tensor(_nm("sp_cab"), [128, 8 * RT], BF16))
        se0, pe0 = se.n, pe.n
        with nc.Block() as blk:
            @blk.scalar
            def _(sc):
                for m in range(8):
                    se.inc(sc.copy(cab[:, m * RT:(m + 1) * RT], CA(m)))
                nz = 0
                if widx < 2:
                    F8 = F8cur["t"]
                    for il in range(K1):
                        for half in range(NRH):
                            sc.wait_ge(pe.h, pe0 + nz + 1)
                            se.inc(sc.copy(
                                sbap(F8, INTER, 0, half * RH * K1 + il, [128, (K1, RH)]),
                                psz[:, :]))
                            nz += 1
                else:
                    for eps in range(hh):
                        for q in range(NQ):
                            for half in range(NRH):
                                sc.wait_ge(pe.h, pe0 + nz + 1)
                                se.inc(sc.copy(
                                    sbap(F8cur["t"], NQ * E, 0,
                                         q * E + eps + half * RH * hh, [128, (hh, RH)]),
                                    psz[:, :]))
                                nz += 1

            @blk.tensor
            def _(t):
                t.wait_ge(se.h, se0 + 8)
                nz = 0
                if widx < 2:
                    for il in range(K1):
                        for half in range(NRH):
                            for m in range(8):
                                mm = t.matmul(psz[:, :],
                                              eg_sb[:, (il * 8 + m) * 128:(il * 8 + m + 1) * 128],
                                              cab[:, m * RT + half * RH: m * RT + half * RH + RH],
                                              start=(m == 0), stop=(m == 7))
                            pe.inc(mm)
                            nz += 1
                            t.wait_ge(se.h, se0 + 8 + nz)
                else:
                    for eps in range(hh):
                        for q in range(NQ):
                            for half in range(NRH):
                                for m in range(8):
                                    mm = t.matmul(psz[:, :],
                                                  ed_sb[:, ((eps * NQ + q) * 8 + m) * 128:
                                                        ((eps * NQ + q) * 8 + m + 1) * 128],
                                                  cab[:, m * RT + half * RH: m * RT + half * RH + RH],
                                                  start=(m == 0), stop=(m == 7))
                                pe.inc(mm)
                                nz += 1
                                t.wait_ge(se.h, se0 + 8 + nz)
        inner.close()
        ssp.close()

    spl = ExitStack()
    php_sb = spl.enter_context(nc.sbuf_tensor(_nm("php_sb"), [128, 2 * RT], F32))
    dmask_sb = spl.enter_context(nc.sbuf_tensor(_nm("dmask_sb"), [128, RT], U8))
    eg_sb = spl.enter_context(nc.sbuf_tensor(_nm("eg_sb"), [128, K1 * 8 * 128], BF16))
    ed_sb = spl.enter_context(nc.sbuf_tensor(_nm("ed_sb"), [128, hh * NQ * 8 * 128], BF16))
    vsel_sb = spl.enter_context(nc.sbuf_tensor(_nm("vsel_sb"), [128, 1024], BF16))
    F8cur = {}

    with nc.Block() as blk:
        @blk.sync
        def _(sy):
            sd.dinc(sy.dma_start(out=php_sb[:, :], in_=phii.ap()[:, :]))
            sd.dinc(sy.dma_start(out=dmask_sb[:, :], in_=dmaski.ap()[:, :]))
            sd.dinc(sy.dma_start(out=vsel_sb[:, :], in_=vseli.ap()[:, :]))
            NEG = K1 * 8
            sd.dinc(sy.dma_start(
                out=AP(eg_sb, 0, [[NEG * 128, 128], [128, NEG], [1, 128]]),
                in_=AP(egi, 0, [[128, 128], [128 * 128, NEG], [1, 128]])))
            NED = hh * NQ * 8
            sd.dinc(sy.dma_start(
                out=AP(ed_sb, 0, [[NED * 128, 128], [128, NED], [1, 128]]),
                in_=AP(edi, 0, [[128, 128], [128 * 128, NED], [1, 128]])))
            sy.wait_ge(sd.h, sd.n)

    HOSTR = HOST["runs"]

    def p0e(widx):
        F8 = F8cur["t"]
        wdst = [gwT, uwT, dwT][widx]
        mask = maskg if widx < 2 else maskd
        NTO = F if widx < 2 else ITILES
        NIB = (INTER if widx < 2 else E) // 512
        OUTC = INTER if widx < 2 else E
        ssp = ExitStack()
        msk = [ssp.enter_context(nc.sbuf_tensor(_nm(f"ev_m{i}"), [128, 512], U8))
               for i in range(4)]
        wtb = [ssp.enter_context(nc.sbuf_tensor(_nm(f"ev_w{i}"), [128, 512], BF16))
               for i in range(4)]
        psa = [ssp.enter_context(nc.psum_tensor(_nm(f"ev_pa{i}"), [128, 512], F32))
               for i in range(2)]
        psb = [ssp.enter_context(nc.psum_tensor(_nm(f"ev_pb{i}"), [128, 512], F32))
               for i in range(2)]
        se0, pe0, ve0 = se.n, pe.n, ve.n
        xsem = [sems["x0"], sems["x1"], sems["r0"], sems["r1"]]
        osem = [sems["g0"], sems["g1"]]
        xval, oval = {}, {}
        NTOT = NTO * NIB
        ob = [osem[0].n, osem[1].n]
        for i in range(NTOT):
            oval[i] = ob[i % 2] + 16 * (i // 2 + 1)
        with nc.Block() as blk:
            @blk.sync
            def _(sy):
                i = 0
                for to in range(NTO):
                    for ib in range(NIB):
                        if i >= 4:
                            sy.wait_ge(ve.h, ve0 + i - 3)
                        xsem[i % 4].dinc(sy.dma_start(
                            out=msk[i % 4][:, :],
                            in_=AP(mask, to * OUTC + ib * 512,
                                   [[NTO * OUTC, 128], [1, 512]])))
                        xval[i] = xsem[i % 4].n
                        i += 1

            @blk.tensor
            def _(t):
                i = 0
                for to in range(NTO):
                    gg = (to % 16) // 4
                    hp = to % 4
                    tp = (32 * gg, 0)
                    qq = to // 16
                    for ib in range(NIB):
                        if i >= 2:
                            t.wait_ge(se.h, se0 + i - 1)
                            t.wait_ge(ve.h, ve0 + i - 1)
                        if widx < 2:
                            rhs_a = F8[32 * gg:32 * gg + 32, ib * 512:(ib + 1) * 512]
                            rhs_b = rhs_a
                        else:
                            rhs_a = F8[32 * gg:32 * gg + 32,
                                       qq * E + ib * 512: qq * E + (ib + 1) * 512]
                            rhs_b = rhs_a
                        pe.inc(t.matmul(psa[i % 2][:, :],
                                        vsel_sb[32 * gg:32 * gg + 32,
                                                (2 * hp) * 128:(2 * hp + 1) * 128],
                                        rhs_a, start=True, stop=True,
                                        tile_position=tp))
                        pe.inc(t.matmul(psb[i % 2][:, :],
                                        vsel_sb[32 * gg:32 * gg + 32,
                                                (2 * hp + 1) * 128:(2 * hp + 2) * 128],
                                        rhs_b, start=True, stop=True,
                                        tile_position=tp))
                        i += 1

            @blk.scalar
            def _(sc):
                i = 0
                for to in range(NTO):
                    for ib in range(NIB):
                        sc.wait_ge(pe.h, pe0 + 2 * i + 1)
                        if i >= 4:
                            sc.wait_ge(osem[i % 2].h, oval[i - 4])
                        se.inc(sc.copy(wtb[i % 4][:, :], psa[i % 2][:, :]))
                        i += 1

            @blk.vector
            def _(v):
                i = 0
                for to in range(NTO):
                    for ib in range(NIB):
                        v.wait_ge(pe.h, pe0 + 2 * i + 2)
                        v.wait_ge(se.h, se0 + i + 1)
                        v.wait_ge(xsem[i % 4].h, xval[i])
                        ve.inc(v.copy_predicated(wtb[i % 4][:, :], msk[i % 4][:, :],
                                                 psb[i % 2][:, :]))
                        i += 1

            @blk.gpsimd
            def _(g):
                i = 0
                for to in range(NTO):
                    for ib in range(NIB):
                        g.wait_ge(ve.h, ve0 + i + 1)
                        if i >= 2:
                            g.wait_ge(osem[i % 2].h, oval[i - 2])
                        osem[i % 2].dinc(g.dma_start(
                            out=AP(wdst, (to * 128) * OUTC + ib * 512,
                                   [[OUTC, 128], [1, 512]]),
                            in_=wtb[i % 4][:, :]))
                        assert oval[i] == osem[i % 2].n
                        i += 1
                g.wait_ge(osem[0].h, osem[0].n)
                g.wait_ge(osem[1].h, osem[1].n)
        ssp.close()

    for widx in range(3):
        wstk = ExitStack()
        cols = INTER if widx < 2 else NQ * E
        F8cur["t"] = wstk.enter_context(nc.sbuf_tensor(_nm("F8"), [128, cols], BF16))
        p0d(widx, HOSTR)
        p0e(widx)
        wstk.close()
    spl.close()
    mrow_sb = es.enter_context(nc.sbuf_tensor(_nm("mrow_sb"), [1, TOK], F32))
    rrow_sb = es.enter_context(nc.sbuf_tensor(_nm("rrow_sb"), [1, TOK], F32))
    trow_sb = es.enter_context(nc.sbuf_tensor(_nm("trow_sb"), [1, TOK], F32))

    class _MockInstr:
        def then_inc(self, *a, **k):
            return self

    class _MockEng:
        def __getattr__(self, name):
            return lambda *a, **k: _MockInstr()

    class P:
        def __init__(self, sems_):
            self.evt = {}
            self.shadow = {}
            self.emit = False
            self.sems = sems_

        def start_record(self):
            self.emit = False
            self.evt = {}
            self.shadow = {k: s.n for k, s in self.sems.items()}

        def start_emit(self):
            self.emit = True
            self.shadow = {k: s.n for k, s in self.sems.items()}

        def inc(self, sname, key, instr=None, k=1):
            if not self.emit:
                self.shadow[sname] += k
                self.evt[key] = (sname, self.shadow[sname])
            else:
                s = self.sems[sname]
                instr.then_inc(s.h, k)
                s.n += k
                assert self.evt[key] == (sname, s.n), (key, self.evt[key], s.n)

        def dinc(self, sname, key, instr=None):
            self.inc(sname, key, instr, 16)

        def wait(self, eng, key):
            if self.emit and key in self.evt:
                sname, val = self.evt[key]
                eng.wait_ge(self.sems[sname].h, val)

    PH = P(sems)

    def run_phase(bodies):
        PH.start_record()
        for name, fn in bodies:
            fn(_MockEng())
        PH.start_emit()
        with nc.Block() as blk:
            for name, fn in bodies:
                getattr(blk, name)(fn)

    def phase_norm(xsrc, tch, xn_sb):
        ssp = ExitStack()
        psm_ = ssp.enter_context(nc.psum_tensor(_nm("ln_pm"), [128, TCH], F32))
        psr_ = ssp.enter_context(nc.psum_tensor(_nm("ln_pr"), [128, TCH], F32))
        xt = [ssp.enter_context(nc.sbuf_tensor(_nm(f"nx{i}"), [128, TCH], F32))
              for i in range(2)]
        t1 = ssp.enter_context(nc.sbuf_tensor(_nm("nt1"), [128, TCH], F32))

        def b_scalar(sc):
            if tch == 0:
                sc.mul(mrow_sb[0:1, :], sumx_sb[0:1, :], 1.0 / E)
                sc.drain()
                sc.square(trow_sb[0:1, :], mrow_sb[0:1, :])
                PH.inc("se", "lnr1", sc.mul(rrow_sb[0:1, :], sumx2_sb[0:1, :], 1.0 / E))
                PH.wait(sc, "lnrv")
                sc.activation(trow_sb[0:1, :], trow_sb[0:1, :], AF.Ln, bias=1e-5)
                sc.drain()
                PH.inc("se", "lnr2",
                       sc.activation(rrow_sb[0:1, :], trow_sb[0:1, :], AF.Exp, scale=-0.5))

        def b_vector(v):
            if tch == 0:
                PH.wait(v, "lnr1")
                PH.inc("ve", "lnrv", v.tensor_tensor(trow_sb[0:1, :], rrow_sb[0:1, :],
                                                     trow_sb[0:1, :], ALU.subtract))
            for f in range(F):
                PH.wait(v, "bcast")
                PH.wait(v, ("nx", f))
                if f > 0:
                    v.drain()
                v.tensor_tensor(t1[:, :], xt[f % 2][:, :], psm_[:, :], ALU.subtract)
                v.drain()
                PH.inc("ve", ("xn", f),
                       v.tensor_tensor(xn_sb[:, f * TCH:(f + 1) * TCH], t1[:, :],
                                       psr_[:, :], ALU.mult))

        def b_tensor(t):
            if tch == 0:
                PH.wait(t, "lnr2")
            for su in range(NSUB):
                t.matmul(psm_[:, su * SUB:(su + 1) * SUB], ones_f[0:1, 0:128],
                         mrow_sb[0:1, tch * TCH + su * SUB: tch * TCH + (su + 1) * SUB],
                         start=True, stop=True)
                mm = t.matmul(psr_[:, su * SUB:(su + 1) * SUB], ones_f[0:1, 0:128],
                              rrow_sb[0:1, tch * TCH + su * SUB: tch * TCH + (su + 1) * SUB],
                              start=True, stop=True)
            PH.inc("pe", "bcast", mm)

        def b_sync(sy):
            sy.wait_ge(sems["o0"].h, sems["o0"].n)
            sy.wait_ge(sems["o1"].h, sems["o1"].n)
            for f in range(F):
                if f >= 2:
                    PH.wait(sy, ("xn", f - 2))
                PH.dinc("x0" if f % 2 == 0 else "x1", ("nx", f),
                        sy.dma_start(out=xt[f % 2][:, :],
                                     in_=xsrc.ap()[f * 128:(f + 1) * 128,
                                                   tch * TCH:(tch + 1) * TCH]))

        run_phase([("scalar", b_scalar), ("vector", b_vector),
                   ("tensor", b_tensor), ("sync", b_sync)])
        ssp.close()

    def phase_qkv(layer, tch, xn_sb):
        rope = (layer == 0)
        NOT = 3 * E // 128
        NOG = NOT // 4
        ssp = ExitStack()
        wbuf = [ssp.enter_context(nc.sbuf_tensor(_nm(f"qw{i}"), [128, F * 512], BF16))
                for i in range(2)]
        qb_bf = [ssp.enter_context(nc.sbuf_tensor(_nm(f"qb{i}"), [128, SUB], BF16))
                 for i in range(2)]
        ro_bf = [ssp.enter_context(nc.sbuf_tensor(_nm(f"ro{i}"), [128, SUB], BF16))
                 for i in range(2)]
        tt1 = ssp.enter_context(nc.sbuf_tensor(_nm("qt1"), [128, SUB], F32))
        tt2 = ssp.enter_context(nc.sbuf_tensor(_nm("qt2"), [128, SUB], F32))
        nsq = [ssp.enter_context(nc.sbuf_tensor(_nm(f"qs{i}"), [128, SUB], BF16))
               for i in range(2)]
        nrow = [ssp.enter_context(nc.sbuf_tensor(_nm(f"nr{i}"), [1, SUB], BF16))
                for i in range(2)]
        ps = [ssp.enter_context(nc.psum_tensor(_nm(f"qp{i}"), [128, SUB], F32))
              for i in range(2)]
        psro = [ssp.enter_context(nc.psum_tensor(_nm(f"qr{i}"), [128, SUB], F32))
                for i in range(2)]
        psn = [ssp.enter_context(nc.psum_tensor(_nm(f"qpn{i}"), [1, SUB], F32))
               for i in range(2)]
        plan = []
        for og in range(NOG):
            for ot4 in range(4):
                for ts in range(NSUB):
                    ot = og * 4 + ot4
                    plan.append(dict(og=og, ot4=ot4, ts=ts, w=ot // F, h=ot % F))

        def b_wload_one(eng, og):
            if og >= 2:
                PH.wait(eng, ("qk", (og - 1) * 4 * NSUB - 1))
            if layer == 0:
                src = AP(wqkvT, og * 512, [[3 * E, 128], [128 * 3 * E, F], [1, 512]])
            else:
                wsel = [wq, wk, wv][(og * 512) // E]
                src = AP(wsel, (og * 512) % E, [[E, 128], [128 * E, F], [1, 512]])
            names = ("w0", "w1") if layer == 0 else ("g0", "g1")
            PH.dinc(names[og % 2], ("wb", og),
                    eng.dma_start(out=AP(
                wbuf[og % 2], 0, [[F * 512, 128], [512, F], [1, 512]]), in_=src))

        def b_wload(eng):
            for og in range(NOG):
                b_wload_one(eng, og)

        def b_tensor(t):
            for i, it in enumerate(plan):
                og, ot4, ts, w, h = it["og"], it["ot4"], it["ts"], it["w"], it["h"]
                PH.wait(t, ("wb", og))
                PH.wait(t, ("qbf", i - 2))
                if rope:
                    PH.wait(t, ("ropet1", i - 2))
                for f in range(F):
                    mm = t.matmul(ps[i % 2][:, :],
                                  wbuf[og % 2][:, f * 512 + ot4 * 128: f * 512 + ot4 * 128 + 128],
                                  xn_sb[:, f * TCH + ts * SUB: f * TCH + (ts + 1) * SUB],
                                  start=(f == 0), stop=(f == F - 1))
                PH.inc("pe", ("qk", i), mm)
                if w < 2 and rope:
                    PH.wait(t, ("qbf", i))
                    PH.wait(t, ("ropet2", i - 2))
                    PH.inc("po", ("rot", i),
                           t.matmul(psro[i % 2][:, :], rmt_sb[:, :], qb_bf[i % 2][:, :],
                                    start=True, stop=True))
                if w < 2:
                    PH.wait(t, ("nsq", i))
                    PH.wait(t, ("nrmevac", i - 2))
                    PH.inc("po", ("nmm", i),
                           t.matmul(psn[i % 2][:, :], ones_b[:, 0:1], nsq[i % 2][:, :],
                                    start=True, stop=True))

        def b_scalar(sc):
            for i, it in enumerate(plan):
                w, h = it["w"], it["h"]
                PH.wait(sc, ("qk", i))
                PH.wait(sc, ("rot", i - 2))
                PH.wait(sc, ("qkvdma", i - 2))
                if layer == 1:
                    op = sc.add(qb_bf[i % 2][:, :], ps[i % 2][:, :],
                                bias_sb[:, w * F + h: w * F + h + 1])
                else:
                    op = sc.copy(qb_bf[i % 2][:, :], ps[i % 2][:, :])
                PH.inc("se", ("qbf", i), op)
                if w < 2:
                    if rope:
                        PH.wait(sc, ("ropeadd", i))
                        src_bf = ro_bf[i % 2]
                    else:
                        src_bf = qb_bf[i % 2]
                        sc.drain()
                    PH.inc("se", ("nsq", i), sc.square(nsq[i % 2][:, :], src_bf[:, :]))
                    PH.wait(sc, ("nmm", i))
                    PH.wait(sc, ("nrmdma", i - 2))
                    PH.inc("se", ("nrmevac", i),
                           sc.copy(nrow[i % 2][0:1, :], psn[i % 2][0:1, :]))

        def b_vector(v):
            if not rope:
                return
            for i, it in enumerate(plan):
                w = it["w"]
                if w >= 2:
                    continue
                ts = it["ts"]
                tsl = slice(tch * TCH + ts * SUB, tch * TCH + (ts + 1) * SUB)
                PH.wait(v, ("qbf", i))
                PH.inc("ve", ("ropet1", i),
                       v.tensor_tensor(tt1[:, :], ps[i % 2][:, :], cos_sb[:, tsl],
                                       ALU.mult))
                PH.wait(v, ("rot", i))
                PH.inc("ve", ("ropet2", i),
                       v.tensor_tensor(tt2[:, :], psro[i % 2][:, :], sin_sb[:, tsl],
                                       ALU.mult))
                PH.wait(v, ("qkvdma", i - 2))
                PH.wait(v, ("nsq", i - 2))
                v.drain()
                PH.inc("ve", ("ropeadd", i),
                       v.tensor_tensor(ro_bf[i % 2][:, :], tt1[:, :], tt2[:, :],
                                       ALU.add))

        def b_store_rng(sy, i0, i1):
            for i in range(i0, i1):
                it = plan[i]
                w, h, ts = it["w"], it["h"], it["ts"]
                tc0 = ts * SUB
                grow = tch * GROWS + h * 3 * D + w * D
                if w < 2 and rope:
                    PH.wait(sy, ("ropeadd", i))
                    src_bf = ro_bf[i % 2]
                else:
                    PH.wait(sy, ("qbf", i))
                    src_bf = qb_bf[i % 2]
                PH.dinc("s0" if i % 2 == 0 else "s1", ("qkvdma", i),
                        sy.dma_start(out=AP(gin, grow * TCH + tc0,
                                            [[TCH, 128], [1, SUB]]),
                                     in_=src_bf[:, :]))
                if w < 2:
                    PH.wait(sy, ("nrmevac", i))
                    PH.dinc("r0" if i % 2 == 0 else "r1", ("nrmdma", i),
                            sy.dma_start(out=AP(gin, (tch * GROWS + NORM0 +
                                                      h * 2 + w) * TCH + tc0,
                                                [[TCH, 1], [1, SUB]]),
                                         in_=nrow[i % 2][0:1, :]))

        def b_store_tail(sy):
            for sn in ["s0", "s1", "r0", "r1"]:
                sy.wait_ge(sems[sn].h, sems[sn].n)

        def b_gather(g):
            PH.wait(g, ("qkvdma", len(plan) - 1))
            PH.wait(g, ("qkvdma", len(plan) - 2))
            nrm_is = [i2 for i2, it2 in enumerate(plan) if it2["w"] < 2]
            PH.wait(g, ("nrmdma", nrm_is[-1]))
            PH.wait(g, ("nrmdma", nrm_is[-2]))
            instr = g.collective_compute(
                "AllGather", ALU.bypass,
                replica_groups=[list(range(NCORES))],
                ins=[AP(gin, tch * GROWS * TCH, [[1, GROWS * TCH]])],
                outs=[AP(G, tch * 8 * GROWS * TCH, [[1, 8 * GROWS * TCH]])])
            if PH.emit:
                instr.then_inc(cc.h)
                cc.n += 1

        if layer == 0:
            def b_sync_all(sy):
                per_og = 4 * NSUB
                for og in range(NOG):
                    b_wload_one(sy, og)
                    b_store_rng(sy, og * per_og, (og + 1) * per_og)
                b_store_tail(sy)
            bodies = [("tensor", b_tensor), ("scalar", b_scalar),
                      ("vector", b_vector), ("sync", b_sync_all),
                      ("gpsimd", b_gather)]
        else:
            def b_sync_all(sy):
                b_store_rng(sy, 0, len(plan))
                b_store_tail(sy)

            def b_gp_all(g):
                b_wload(g)
                b_gather(g)
            bodies = [("tensor", b_tensor), ("scalar", b_scalar),
                      ("vector", b_vector), ("sync", b_sync_all),
                      ("gpsimd", b_gp_all)]
        run_phase(bodies)
        ssp.close()

    def phase_gather(src_t, dst_t):
        with nc.Block() as blk:
            @blk.gpsimd
            def _(g):
                g.collective_compute(
                    "AllGather", ALU.bypass,
                    replica_groups=[list(range(NCORES))],
                    ins=[src_t.ap()[:, :]],
                    outs=[dst_t.ap()[:, :]],
                ).then_inc(cc.h)
                cc.n += 1
                g.wait_ge(cc.h, cc.n)

    def phase_stage_attn():
        with nc.Block() as blk:
            @blk.sync
            def _(sy):
                sy.wait_ge(cc.h, cc.n)
                for rr in range(4):
                    sd.dinc(sy.dma_start(
                        out=AP(GL, rr * NHG * 3 * D * TOK,
                               [[TCH, NT], [TOK, NHG * 3 * D], [1, TCH]]),
                        in_=AP(G, regs["g"] + rr * GROWS * TCH,
                               [[8 * GROWS * TCH, NT], [1, NHG * 3 * D * TCH]])))
                    sd.dinc(sy.dma_start(
                        out=AP(NL, rr * 2 * NHG * TOK,
                               [[TCH, NT], [TOK, 2 * NHG], [1, TCH]]),
                        in_=AP(G, regs["nq"] + rr * GROWS * TCH,
                               [[8 * GROWS * TCH, NT], [1, 2 * NHG * TCH]])))
                sy.wait_ge(sd.h, sd.n)

    def phase_stage_out():
        with nc.Block() as blk:
            @blk.sync
            def _(sy):
                sy.wait_ge(cc.h, cc.n)
                for hp2 in range(NHG):
                    sd.dinc(sy.dma_start(
                        out=AP(AOL, hp2 * 4 * D * TOK, [[TOK, 4 * D], [1, TOK]]),
                        in_=AP(GO, regs["go"] + hp2 * 8 * D * S,
                               [[S, 4 * D], [1, TOK]])))
                sy.wait_ge(sd.h, sd.n)

    def phase_attn(layer):
        sm = C["SM_SCALE"]
        NQUAD = S // 512
        NKQB = S // QB
        NVJ = QB // 128
        setst = ExitStack()
        mhat_all = setst.enter_context(nc.sbuf_tensor(_nm("at_mha"), [128, NHG], F32))
        sustk = ExitStack()
        nall = sustk.enter_context(nc.sbuf_tensor(_nm("at_nall"), [8 * NHG, TOK], BF16))
        rmax = sustk.enter_context(nc.sbuf_tensor(_nm("at_rmax"), [8 * NHG, 1], F32))
        nmt32 = sustk.enter_context(nc.sbuf_tensor(_nm("at_nmt"), [1, 8 * NHG], F32))
        sca8 = sustk.enter_context(nc.sbuf_tensor(_nm("at_sca8"), [1, 2 * NHG], F32))
        mrow4 = sustk.enter_context(nc.sbuf_tensor(_nm("at_mr4"), [1, NHG], F32))
        ps_su = sustk.enter_context(nc.psum_tensor(_nm("at_psu"), [128, 32], F32))

        def b_su_sync(sy):
            PH.dinc("kd", "nld", sy.dma_start(out=nall[:, :], in_=NL.ap()[:, :]))

        def b_su_vector(v):
            PH.wait(v, "nld")
            PH.inc("ve", "rmx", v.reduce_max(rmax[:, 0:1], nall[:, :],
                                             axis=mybir.AxisListType.X))
            PH.wait(v, "sntr")
            v.tensor_copy(nmt32[0:1, :], ps_su[0:1, 0:8 * NHG])
            v.drain()
            for h2 in range(NHG):
                for w2 in range(2):
                    op = v.reduce_max(sca8[0:1, h2 * 2 + w2:h2 * 2 + w2 + 1],
                                      sbap(nmt32, 8 * NHG, 0, h2 * 2 + w2,
                                           [1, [2 * NHG, 4]]),
                                      axis=mybir.AxisListType.X)
            PH.inc("ve", "s8", op)
            PH.wait(v, "sq8")
            for h2 in range(NHG):
                op = v.tensor_tensor(mrow4[0:1, h2:h2 + 1],
                                     sca8[0:1, 2 * h2:2 * h2 + 1],
                                     sca8[0:1, 2 * h2 + 1:2 * h2 + 2], ALU.mult)
            PH.inc("ve", "mr4", op)
            PH.wait(v, "mbc4")
            PH.inc("ve", "mhall", v.tensor_copy(mhat_all[:, :], ps_su[:, 0:NHG]))

        def b_su_tensor(t):
            PH.wait(t, "rmx")
            PH.inc("po", "sntr", t.transpose(ps_su[0:1, 0:8 * NHG], rmax[:, 0:1],
                                             ident_f[0:8 * NHG, 0:8 * NHG]))
            PH.wait(t, "mh4")
            PH.inc("po", "mbc4", t.matmul(ps_su[:, 0:NHG], ones_f[0:1, 0:128],
                                          mrow4[0:1, :], start=True, stop=True))

        def b_su_scalar(sc):
            PH.wait(sc, "s8")
            PH.inc("se", "sq8", sc.sqrt(sca8[0:1, :], sca8[0:1, :]))
            PH.wait(sc, "mr4")
            PH.inc("se", "mh4", sc.mul(mrow4[0:1, :], mrow4[0:1, :], -sm * 1.02))

        run_phase([("sync", b_su_sync), ("vector", b_su_vector),
                   ("tensor", b_su_tensor), ("scalar", b_su_scalar)])
        sustk.close()

        for hp in range(NHG):
            asp = ExitStack()
            q_sb = asp.enter_context(nc.sbuf_tensor(_nm("at_q"), [128, S], BF16))
            k_sb = asp.enter_context(nc.sbuf_tensor(_nm("at_k"), [128, S], BF16))
            vt_sb = asp.enter_context(nc.sbuf_tensor(_nm("at_vt"), [128, S], BF16))
            vtmp = asp.enter_context(nc.sbuf_tensor(_nm("at_vm"), [128, QB], BF16))
            p_bf = [asp.enter_context(nc.sbuf_tensor(_nm(f"at_p{i}"), [128, 512], BF16))
                    for i in range(4)]
            lrow = [asp.enter_context(nc.sbuf_tensor(_nm(f"at_lr{i}"), [1, 512], F32))
                    for i in range(2)]
            lacc = [asp.enter_context(nc.sbuf_tensor(_nm(f"at_la{i}"), [128, 512], F32))
                    for i in range(2)]
            lb_sb = [asp.enter_context(nc.sbuf_tensor(_nm(f"at_lb{i}"), [128, 512], F32))
                     for i in range(2)]
            ob_bf = [asp.enter_context(nc.sbuf_tensor(_nm(f"at_o{i}"), [128, 512], BF16))
                     for i in range(2)]
            ps_st = [asp.enter_context(nc.psum_tensor(_nm(f"at_ps{i}"), [128, 512], F32))
                     for i in range(2)]
            ps_oT = [asp.enter_context(nc.psum_tensor(_nm(f"at_po{i}"), [128, 512], F32))
                     for i in range(2)]
            ps_l = [asp.enter_context(nc.psum_tensor(_nm(f"at_pl{i}"), [1, 512], F32))
                    for i in range(2)]
            ps_lb = asp.enter_context(nc.psum_tensor(_nm("at_plb"), [128, 512], F32))
            ps_vtm = asp.enter_context(nc.psum_tensor(_nm("at_pv"), [128, 256], BF16))
            ps_vt = [ps_vtm[:, 0:128], ps_vtm[:, 128:256]]

            iters = []
            for qq in range(NQUAD):
                for j in range(4 * qq + 4):
                    iters.append(dict(qq=qq, j=j, first=(j == 0),
                                      last=(j == 4 * qq + 3), doff=j - 4 * qq))
            NI = len(iters)
            vt_sched = [[] for _ in range(NQUAD)]
            built = 0
            for qq in range(NQUAD):
                need = (4 * qq + 3) // NVJ + 1
                while built < need:
                    vt_sched[qq].append(built)
                    built += 1

            def b_sync(sy):
                for rr in range(4):
                    PH.dinc("kd", ("qld", rr), sy.dma_start(
                        out=q_sb[:, rr * TOK:(rr + 1) * TOK],
                        in_=AP(GL, ((rr * NHG + hp) * 3 * D) * TOK,
                               [[TOK, 128], [1, TOK]])))
                    PH.dinc("kd", ("kld", rr), sy.dma_start(
                        out=k_sb[:, rr * TOK:(rr + 1) * TOK],
                        in_=AP(GL, ((rr * NHG + hp) * 3 * D + D) * TOK,
                               [[TOK, 128], [1, TOK]])))

                def ob_store(qx):
                    PH.wait(sy, ("ob", qx))
                    PH.dinc("o0" if qx % 2 == 0 else "o1", ("obdma", qx),
                            sy.dma_start(
                        out=AP(goin, (hp * D) * S + qx * 512, [[S, 128], [1, 512]]),
                        in_=ob_bf[qx % 2][:, :]))
                def vld(m):
                    PH.wait(sy, ("vtt", m * NVJ - 1))
                    rr = (m * QB) // TOK
                    co = (m * QB) % TOK
                    PH.dinc("vd", ("vld", m), sy.dma_start(
                        out=vtmp[:, :],
                        in_=AP(GL, ((rr * NHG + hp) * 3 * D + 2 * D) * TOK + co,
                               [[TOK, 128], [1, QB]])))
                for m in vt_sched[0] + (vt_sched[1] if NQUAD > 1 else []):
                    vld(m)
                for qq in range(NQUAD):
                    if qq + 2 < NQUAD:
                        for m in vt_sched[qq + 2]:
                            vld(m)
                    ob_store(qq)
                for sn in ["o0", "o1"]:
                    sy.wait_ge(sems[sn].h, sems[sn].n)

            def b_tensor(t):
                PH.wait(t, ("kld", 3))

                def emit_lav(ii):
                    it2 = iters[ii]
                    qq2, j2 = it2["qq"], it2["j"]
                    PH.wait(t, ("exp", ii))
                    if it2["first"]:
                        PH.wait(t, ("lrd", qq2 - 2))
                        PH.wait(t, ("ob", qq2 - 2))
                    PH.wait(t, ("vte", j2))
                    PH.inc("pe", ("av", ii),
                           t.matmul(ps_oT[qq2 % 2][:, :],
                                    vt_sb[:, j2 * 128:(j2 + 1) * 128],
                                    p_bf[ii % 4][:, :],
                                    start=it2["first"], stop=it2["last"]))
                    if it2["last"]:
                        PH.wait(t, ("lac", ii))
                        PH.inc("sp", ("lmmq", qq2),
                               t.matmul(ps_l[qq2 % 2][0:1, :], ones_f[:, 0:1],
                                        lacc[qq2 % 2][:, :], start=True, stop=True))
                        PH.wait(t, ("lrd", qq2))
                        PH.wait(t, ("lbe", qq2 - 1))
                        PH.inc("po", ("lbc", qq2),
                               t.matmul(ps_lb[:, :], ones_f[0:1, 0:128],
                                        lrow[qq2 % 2][0:1, :],
                                        start=True, stop=True))

                for i, it in enumerate(iters):
                    qq, j = it["qq"], it["j"]
                    if it["first"]:
                        for m in vt_sched[qq]:
                            PH.wait(t, ("vld", m))
                            for vj in range(NVJ):
                                gvj = m * NVJ + vj
                                PH.wait(t, ("vte", gvj - 1))
                                PH.inc("pw", ("vtt", gvj),
                                       t.transpose(ps_vt[gvj % 2][:, :],
                                                   vtmp[:, vj * 128:(vj + 1) * 128],
                                                   ident_b[:, :]))
                    PH.wait(t, ("exp", i - 2))
                    diag = it["doff"] >= 0
                    mm = t.matmul(ps_st[i % 2][:, :], k_sb[:, j * 128:(j + 1) * 128],
                                  q_sb[:, qq * 512:(qq + 1) * 512],
                                  start=True, stop=not diag)
                    if diag:
                        mm = t.matmul(ps_st[i % 2][:, :], ident_b[:, :],
                                      dmt_sb[:, it["doff"] * 512:
                                             (it["doff"] + 1) * 512],
                                      start=False, stop=True)
                    PH.inc("pe", ("qkt", i), mm)
                    if i - 2 >= 0:
                        emit_lav(i - 2)
                emit_lav(NI - 2)
                emit_lav(NI - 1)

            def b_scalar(sc):
                def lbe_emit(ii):
                    it2 = iters[ii]
                    if not it2["last"]:
                        return
                    qq2 = it2["qq"]
                    PH.wait(sc, ("lbc", qq2))
                    PH.wait(sc, ("ob", qq2 - 2))
                    PH.inc("se", ("lbe", qq2),
                           sc.copy(lb_sb[qq2 % 2][:, :], ps_lb[:, :]))

                for i, it in enumerate(iters):
                    if it["first"]:
                        for m in vt_sched[it["qq"]]:
                            for vj in range(NVJ):
                                gvj = m * NVJ + vj
                                PH.wait(sc, ("vtt", gvj))
                                PH.inc("se", ("vte", gvj),
                                       sc.copy(vt_sb[:, gvj * 128:(gvj + 1) * 128],
                                               ps_vt[gvj % 2]))
                    PH.wait(sc, ("qkt", i))
                    PH.wait(sc, ("av", i - 4))
                    PH.wait(sc, ("lac", i - 4))
                    PH.inc("se", ("exp", i),
                           sc.activation(p_bf[i % 4][:, :], ps_st[i % 2][:, :], AF.Exp,
                                         bias=mhat_all[:, hp:hp + 1], scale=sm))
                    if i - 2 >= 0:
                        lbe_emit(i - 2)
                lbe_emit(NI - 2)
                lbe_emit(NI - 1)

            def b_vector(v):
                def qend_emit(ii):
                    it2 = iters[ii]
                    if not it2["last"]:
                        return
                    qq2 = it2["qq"]
                    PH.wait(v, ("lmmq", qq2))
                    PH.inc("vf", ("lrd", qq2),
                           v.reciprocal(lrow[qq2 % 2][0:1, :], ps_l[qq2 % 2][0:1, :]))
                    PH.wait(v, ("lbe", qq2))
                    PH.wait(v, ("av", ii))
                    PH.wait(v, ("obdma", qq2 - 2))
                    v.drain()
                    PH.inc("vf", ("ob", qq2),
                           v.tensor_tensor(ob_bf[qq2 % 2][:, :], ps_oT[qq2 % 2][:, :],
                                           lb_sb[qq2 % 2][:, :], ALU.mult))

                for i, it in enumerate(iters):
                    qq = it["qq"]
                    PH.wait(v, ("exp", i))
                    v.drain()
                    if it["first"]:
                        PH.wait(v, ("lmmq", qq - 2))
                        op = v.tensor_copy(lacc[qq % 2][:, :], p_bf[i % 4][:, :])
                    else:
                        op = v.tensor_tensor(lacc[qq % 2][:, :], lacc[qq % 2][:, :],
                                             p_bf[i % 4][:, :], ALU.add)
                    PH.inc("ve", ("lac", i), op)
                    if i - 2 >= 0:
                        qend_emit(i - 2)
                qend_emit(NI - 2)
                qend_emit(NI - 1)

            def b_gp(g):
                PH.wait(g, ("obdma", NQUAD - 1))
                PH.wait(g, ("obdma", NQUAD - 2))
                instr = g.collective_compute(
                    "AllGather", ALU.bypass,
                    replica_groups=[list(range(NCORES))],
                    ins=[AP(goin, hp * D * S, [[1, D * S]])],
                    outs=[AP(GO, hp * 8 * D * S, [[1, 8 * D * S]])])
                if PH.emit:
                    instr.then_inc(cc.h)
                    cc.n += 1

            run_phase([("sync", b_sync), ("tensor", b_tensor),
                       ("scalar", b_scalar), ("vector", b_vector),
                       ("gpsimd", b_gp)])
            asp.close()
        setst.close()


    def phase_out(layer, tch, xsrc, xdst, stats):
        NOG = E // 512
        ssp = ExitStack()
        wbuf = [ssp.enter_context(nc.sbuf_tensor(_nm(f"ow{i}"), [128, F * 512], BF16))
                for i in range(2)]
        ao_sb = ssp.enter_context(nc.sbuf_tensor(_nm("ao_sb"), [128, F * TCH], BF16))
        xold = [ssp.enter_context(nc.sbuf_tensor(_nm(f"oxo{i}"), [128, SUB], F32))
                for i in range(2)]
        xnew = [ssp.enter_context(nc.sbuf_tensor(_nm(f"oxn{i}"), [128, SUB], F32))
                for i in range(2)]
        xbf = [ssp.enter_context(nc.sbuf_tensor(_nm(f"oxb{i}"), [128, SUB], BF16))
               for i in range(2)]
        x2bf = [ssp.enter_context(nc.sbuf_tensor(_nm(f"ox2{i}"), [128, SUB], BF16))
                for i in range(2)]
        ps = [ssp.enter_context(nc.psum_tensor(_nm(f"op{i}"), [128, SUB], F32))
              for i in range(2)]
        psx = [ssp.enter_context(nc.psum_tensor(_nm(f"opx{i}"), [1, SUB], F32))
               for i in range(NSUB)]
        psx2 = [ssp.enter_context(nc.psum_tensor(_nm(f"opx2{i}"), [1, SUB], F32))
                for i in range(NSUB)]
        plan = []
        for og in range(NOG):
            for ot4 in range(4):
                for ts in range(NSUB):
                    plan.append(dict(og=og, ot=og * 4 + ot4, ot4=ot4, ts=ts))
        NI = len(plan)
        o_pre = (sems["o0"].n, sems["o1"].n)

        def b_loads_head(sy):
            for f in range(F):
                rowc = ((f % NHG) * 4 * D + (f // NHG) * D) * TOK
                PH.dinc("kd", ("ao", f), sy.dma_start(
                    out=ao_sb[:, f * TCH:(f + 1) * TCH],
                    in_=AP(AOL, rowc + tch * TCH, [[TOK, 128], [1, TCH]])))

        def b_loads_rng(sy, i0, i1):
            for i in range(i0, i1):
                it = plan[i]
                PH.wait(sy, ("res", i - 2))
                PH.dinc("x0" if i % 2 == 0 else "x1", ("xold", i), sy.dma_start(
                    out=xold[i % 2][:, :],
                    in_=xsrc.ap()[it["ot"] * 128:(it["ot"] + 1) * 128,
                                  tch * TCH + it["ts"] * SUB:
                                  tch * TCH + (it["ts"] + 1) * SUB]))
                PH.wait(sy, ("x2b", i) if stats else ("res", i))
                PH.dinc("o0" if i % 2 == 0 else "o1", ("xnew", i), sy.dma_start(
                    out=AP(xdst, (it["ot"] * 128) * TOK + tch * TCH + it["ts"] * SUB,
                           [[TOK, 128], [1, SUB]]),
                    in_=xnew[i % 2][:, :]))

        def b_wload_one(eng, og):
            if og >= 2:
                PH.wait(eng, ("pmm", (og - 1) * 4 * NSUB - 1))
            if layer == 0:
                src = AP(woutT, og * 512, [[E, 128], [128 * E, F], [1, 512]])
            else:
                src = AP(wo, og * 512, [[E, 128], [128 * E, F], [1, 512]])
            names = ("w0", "w1") if layer == 0 else ("g0", "g1")
            PH.dinc(names[og % 2], ("wb", og), eng.dma_start(out=AP(
                wbuf[og % 2], 0, [[F * 512, 128], [512, F], [1, 512]]), in_=src))

        def b_wload(eng):
            for og in range(NOG):
                b_wload_one(eng, og)

        def b_tensor(t):
            for i, it in enumerate(plan):
                og, ot4, ts = it["og"], it["ot4"], it["ts"]
                PH.wait(t, ("wb", og))
                PH.wait(t, ("ao", F - 1))
                PH.wait(t, ("res", i - 2))
                for f in range(F):
                    mm = t.matmul(ps[i % 2][:, :],
                                  wbuf[og % 2][:, f * 512 + ot4 * 128: f * 512 + ot4 * 128 + 128],
                                  ao_sb[:, f * TCH + ts * SUB: f * TCH + (ts + 1) * SUB],
                                  start=(f == 0), stop=(f == F - 1))
                PH.inc("pe", ("pmm", i), mm)
                if stats:
                    PH.wait(t, ("x2b", i))
                    mm1 = t.matmul(psx[ts][0:1, :], ones_b[:, 0:1], xbf[i % 2][:, :],
                                   start=(it["ot"] == 0), stop=(it["ot"] == 4 * NOG - 1))
                    mm2 = t.matmul(psx2[ts][0:1, :], ones_b[:, 0:1], x2bf[i % 2][:, :],
                                   start=(it["ot"] == 0), stop=(it["ot"] == 4 * NOG - 1))
                    PH.inc("po", ("smm", i), mm2)

        def b_vector(v):
            v.wait_ge(sems["o0"].h, o_pre[0])
            v.wait_ge(sems["o1"].h, o_pre[1])
            for i, it in enumerate(plan):
                PH.wait(v, ("pmm", i))
                PH.wait(v, ("xold", i))
                PH.wait(v, ("xnew", i - 2))
                if stats:
                    PH.wait(v, ("x2b", i - 2))
                PH.inc("ve", ("res", i),
                       v.tensor_tensor(xnew[i % 2][:, :], xold[i % 2][:, :],
                                       ps[i % 2][:, :], ALU.add))
            if stats:
                for ts in range(NSUB):
                    PH.wait(v, ("smm", NI - 1))
                    v.tensor_copy(sumx_sb[0:1, tch * TCH + ts * SUB:
                                          tch * TCH + (ts + 1) * SUB], psx[ts][0:1, :])
                    PH.inc("ve", ("sev", ts),
                           v.tensor_copy(sumx2_sb[0:1, tch * TCH + ts * SUB:
                                                  tch * TCH + (ts + 1) * SUB],
                                         psx2[ts][0:1, :]))

        def b_scalar(sc):
            if not stats:
                return
            for i, it in enumerate(plan):
                PH.wait(sc, ("res", i))
                PH.wait(sc, ("smm", i - 2))
                PH.inc("se", ("xbf", i), sc.copy(xbf[i % 2][:, :], xnew[i % 2][:, :]))
                PH.inc("se", ("x2b", i), sc.square(x2bf[i % 2][:, :], xnew[i % 2][:, :]))

        if layer == 0:
            def b_sync_all(sy):
                b_loads_head(sy)
                per_og = 4 * NSUB
                for og in range(NOG):
                    b_wload_one(sy, og)
                    b_loads_rng(sy, og * per_og, (og + 1) * per_og)
        else:
            def b_sync_all(sy):
                b_loads_head(sy)
                b_loads_rng(sy, 0, len(plan))
        if layer == 0:
            bodies = [("tensor", b_tensor), ("vector", b_vector),
                      ("scalar", b_scalar), ("sync", b_sync_all)]
        else:
            bodies = [("tensor", b_tensor), ("vector", b_vector),
                      ("scalar", b_scalar), ("sync", b_sync_all), ("gpsimd", b_wload)]
        run_phase(bodies)
        ssp.close()

    def phase_ffn_gu(tch, xn_sb, inter_sb):
        NITP = ITILES // 2
        ssp = ExitStack()
        wg = [ssp.enter_context(nc.sbuf_tensor(_nm(f"fg{i}"), [128, F * 256], BF16))
              for i in range(2)]
        wu = [ssp.enter_context(nc.sbuf_tensor(_nm(f"fu{i}"), [128, F * 256], BF16))
              for i in range(2)]
        g_bf = [ssp.enter_context(nc.sbuf_tensor(_nm(f"fgb{i}"), [128, SUB], BF16))
                for i in range(2)]
        u_bf = [ssp.enter_context(nc.sbuf_tensor(_nm(f"fub{i}"), [128, SUB], BF16))
                for i in range(2)]
        psg = [ssp.enter_context(nc.psum_tensor(_nm(f"fpg{i}"), [128, SUB], F32))
               for i in range(2)]
        psu = [ssp.enter_context(nc.psum_tensor(_nm(f"fpu{i}"), [128, SUB], F32))
               for i in range(2)]
        plan = []
        for itp in range(NITP):
            for itt in range(2):
                for ts in range(NSUB):
                    plan.append(dict(itp=itp, it=itp * 2 + itt, itt=itt, ts=ts))
        NI = len(plan)

        def b_wload(sy):
            for itp in range(NITP):
                if itp >= 2:
                    PH.wait(sy, ("umm", (itp - 1) * 2 * NSUB - 1))
                sn = "w0" if itp % 2 == 0 else "w1"
                PH.dinc(sn, ("wg", itp), sy.dma_start(
                    out=AP(wg[itp % 2], 0, [[F * 256, 128], [256, F], [1, 256]]),
                    in_=AP(gwT, itp * 256, [[INTER, 128], [128 * INTER, F], [1, 256]])))
                PH.dinc(sn, ("wu", itp), sy.dma_start(
                    out=AP(wu[itp % 2], 0, [[F * 256, 128], [256, F], [1, 256]]),
                    in_=AP(uwT, itp * 256, [[INTER, 128], [128 * INTER, F], [1, 256]])))

        def b_tensor(t):
            for i, it in enumerate(plan):
                itp, itt, ts = it["itp"], it["itt"], it["ts"]
                PH.wait(t, ("wu", itp))
                PH.wait(t, ("gbf", i - 2))
                for f in range(F):
                    mm = t.matmul(psg[i % 2][:, :],
                                  wg[itp % 2][:, f * 256 + itt * 128: f * 256 + itt * 128 + 128],
                                  xn_sb[:, f * TCH + ts * SUB: f * TCH + (ts + 1) * SUB],
                                  start=(f == 0), stop=(f == F - 1))
                PH.inc("pe", ("gmm", i), mm)
                PH.wait(t, ("ubf", i - 2))
                for f in range(F):
                    mm = t.matmul(psu[i % 2][:, :],
                                  wu[itp % 2][:, f * 256 + itt * 128: f * 256 + itt * 128 + 128],
                                  xn_sb[:, f * TCH + ts * SUB: f * TCH + (ts + 1) * SUB],
                                  start=(f == 0), stop=(f == F - 1))
                PH.inc("pe", ("umm", i), mm)

        def b_scalar(sc):
            for i, it in enumerate(plan):
                PH.wait(sc, ("gmm", i))
                PH.wait(sc, ("imul", i - 2))
                PH.inc("se", ("gbf", i),
                       sc.activation(g_bf[i % 2][:, :], psg[i % 2][:, :], AF.Silu))
                PH.wait(sc, ("umm", i))
                PH.inc("se", ("ubf", i), sc.copy(u_bf[i % 2][:, :], psu[i % 2][:, :]))

        def b_vector(v):
            for i, it in enumerate(plan):
                PH.wait(v, ("ubf", i))
                PH.inc("ve", ("imul", i),
                       v.tensor_tensor(inter_sb[:, it["it"] * TCH + it["ts"] * SUB:
                                                it["it"] * TCH + (it["ts"] + 1) * SUB],
                                       g_bf[i % 2][:, :], u_bf[i % 2][:, :], ALU.mult))

        run_phase([("sync", b_wload), ("tensor", b_tensor),
                   ("scalar", b_scalar), ("vector", b_vector)])
        ssp.close()

    def phase_ffn_down(tch, inter_sb, xsrc, xdst, stats, final):
        NOG2 = E // 256
        ssp = ExitStack()
        wdn = [ssp.enter_context(nc.sbuf_tensor(_nm(f"fd{i}"), [128, ITILES * 256], BF16))
               for i in range(2)]
        xold = [ssp.enter_context(nc.sbuf_tensor(_nm(f"dxo{i}"), [128, SUB], F32))
                for i in range(2)]
        xnew = [ssp.enter_context(nc.sbuf_tensor(_nm(f"dxn{i}"), [128, SUB], F32))
                for i in range(2)]
        xbf = [ssp.enter_context(nc.sbuf_tensor(_nm(f"dxb{i}"), [128, SUB], BF16))
               for i in range(2)]
        x2bf = [ssp.enter_context(nc.sbuf_tensor(_nm(f"dx2{i}"), [128, SUB], BF16))
                for i in range(2)]
        ps = [ssp.enter_context(nc.psum_tensor(_nm(f"dp{i}"), [128, SUB], F32))
              for i in range(2)]
        psx = [ssp.enter_context(nc.psum_tensor(_nm(f"dpx{i}"), [1, SUB], F32))
               for i in range(NSUB)]
        psx2 = [ssp.enter_context(nc.psum_tensor(_nm(f"dpx2{i}"), [1, SUB], F32))
                for i in range(NSUB)]
        plan = []
        for og in range(NOG2):
            for ot2 in range(2):
                for ts in range(NSUB):
                    plan.append(dict(og=og, ot=og * 2 + ot2, ot2=ot2, ts=ts))
        NI = len(plan)
        o_pre = (sems["o0"].n, sems["o1"].n)

        def b_wload_one(sy, og):
            if og >= 2:
                PH.wait(sy, ("dmm", (og - 1) * 2 * NSUB - 1))
            PH.dinc("w0" if og % 2 == 0 else "w1", ("wd", og), sy.dma_start(
                out=AP(wdn[og % 2], 0, [[ITILES * 256, 128], [256, ITILES], [1, 256]]),
                in_=AP(dwT, og * 256, [[E, 128], [128 * E, ITILES], [1, 256]])))

        def b_io_rng(sy, i0, i1):
            for i in range(i0, i1):
                it = plan[i]
                PH.wait(sy, ("res", i - 2))
                PH.dinc("x0" if i % 2 == 0 else "x1", ("xold", i), sy.dma_start(
                    out=xold[i % 2][:, :],
                    in_=xsrc.ap()[it["ot"] * 128:(it["ot"] + 1) * 128,
                                  tch * TCH + it["ts"] * SUB:
                                  tch * TCH + (it["ts"] + 1) * SUB]))
                if stats:
                    PH.wait(sy, ("x2b", i))
                else:
                    PH.wait(sy, ("res", i))
                PH.dinc("o0" if i % 2 == 0 else "o1", ("xnew", i), sy.dma_start(
                    out=AP(xdst, (it["ot"] * 128) * TOK + tch * TCH + it["ts"] * SUB,
                           [[TOK, 128], [1, SUB]]),
                    in_=xnew[i % 2][:, :]))

        def b_sync(sy):
            per_og = 2 * NSUB
            for og in range(NOG2):
                b_wload_one(sy, og)
                b_io_rng(sy, og * per_og, (og + 1) * per_og)
            for sn in ["o0", "o1"]:
                sy.wait_ge(sems[sn].h, sems[sn].n)

        def b_tensor(t):
            for i, it in enumerate(plan):
                og, ot2, ts = it["og"], it["ot2"], it["ts"]
                PH.wait(t, ("wd", og))
                PH.wait(t, ("res", i - 2))
                for fi in range(ITILES):
                    mm = t.matmul(ps[i % 2][:, :],
                                  wdn[og % 2][:, fi * 256 + ot2 * 128: fi * 256 + ot2 * 128 + 128],
                                  inter_sb[:, fi * TCH + ts * SUB: fi * TCH + (ts + 1) * SUB],
                                  start=(fi == 0), stop=(fi == ITILES - 1))
                PH.inc("pe", ("dmm", i), mm)
                if stats:
                    PH.wait(t, ("x2b", i))
                    t.matmul(psx[ts][0:1, :], ones_b[:, 0:1], xbf[i % 2][:, :],
                             start=(it["ot"] == 0), stop=(it["ot"] == 2 * NOG2 - 1))
                    mm2 = t.matmul(psx2[ts][0:1, :], ones_b[:, 0:1], x2bf[i % 2][:, :],
                                   start=(it["ot"] == 0), stop=(it["ot"] == 2 * NOG2 - 1))
                    PH.inc("po", ("smm", i), mm2)

        def b_vector(v):
            v.wait_ge(sems["o0"].h, o_pre[0])
            v.wait_ge(sems["o1"].h, o_pre[1])
            for i, it in enumerate(plan):
                PH.wait(v, ("dmm", i))
                PH.wait(v, ("xold", i))
                PH.wait(v, ("xnew", i - 2))
                if stats:
                    PH.wait(v, ("x2b", i - 2))
                PH.inc("ve", ("res", i),
                       v.tensor_tensor(xnew[i % 2][:, :], xold[i % 2][:, :],
                                       ps[i % 2][:, :], ALU.add))
            if stats:
                for ts in range(NSUB):
                    PH.wait(v, ("smm", NI - 1))
                    v.tensor_copy(sumx_sb[0:1, tch * TCH + ts * SUB:
                                          tch * TCH + (ts + 1) * SUB], psx[ts][0:1, :])
                    PH.inc("ve", ("sev", ts),
                           v.tensor_copy(sumx2_sb[0:1, tch * TCH + ts * SUB:
                                                  tch * TCH + (ts + 1) * SUB],
                                         psx2[ts][0:1, :]))

        def b_scalar(sc):
            if not stats:
                return
            for i, it in enumerate(plan):
                PH.wait(sc, ("res", i))
                PH.wait(sc, ("smm", i - 2))
                PH.inc("se", ("xbf", i), sc.copy(xbf[i % 2][:, :], xnew[i % 2][:, :]))
                PH.inc("se", ("x2b", i), sc.square(x2bf[i % 2][:, :], xnew[i % 2][:, :]))

        run_phase([("sync", b_sync), ("tensor", b_tensor),
                   ("vector", b_vector), ("scalar", b_scalar)])
        ssp.close()

    def sublayer_attn(layer, xsrc, xdst, stats):
        asp = ExitStack()
        xn_sb = asp.enter_context(nc.sbuf_tensor(_nm("sx_xn"), [128, F * TCH], BF16))
        for tch in range(NT):
            phase_norm(xsrc, tch, xn_sb)
            phase_qkv(layer, tch, xn_sb)
        asp.close()
        phase_stage_attn()
        phase_attn(layer)
        phase_stage_out()
        asp = ExitStack()
        for tch in range(NT):
            phase_out(layer, tch, xsrc, xdst, stats)
        asp.close()

    def sublayer_ffn(xsrc, xdst, stats, final):
        for tch in range(NT):
            asp = ExitStack()
            inter_sb = asp.enter_context(
                nc.sbuf_tensor(_nm("sf_in"), [128, ITILES * TCH], BF16))
            xstk = ExitStack()
            xn_sb = xstk.enter_context(nc.sbuf_tensor(_nm("sf_xn"), [128, F * TCH], BF16))
            phase_norm(xsrc, tch, xn_sb)
            phase_ffn_gu(tch, xn_sb, inter_sb)
            xstk.close()
            phase_ffn_down(tch, inter_sb, xsrc, xdst, stats,
                           final and tch == NT - 1)
            asp.close()

    sublayer_attn(0, xin, x1, stats=True)
    sublayer_ffn(x1, x2, stats=True, final=False)
    sublayer_attn(1, x2, x3, stats=True)
    sublayer_ffn(x3, xout, stats=False, final=True)
    if KDBG:
        with nc.Block() as blk:
            @blk.sync
            def _(sy):
                for dst, srct in [(dbg1, x1), (dbg2, x2), (dbg3, x3)]:
                    od.dinc(sy.dma_start(out=dst.ap()[:, :], in_=srct.ap()[:, :]))
                sy.wait_ge(od.h, od.n)

    es.close()
    nc.finalize()
    return nc



def host_static(C):
    g = spline_grid(C)
    mg, md = spline_masks(C, g)
    vsel = spline_vsel(C, g)
    Eg, Ed = spline_eperm(C)
    runs = mload_runs(C, g)
    return dict(grid=g, maskg=mg, maskd=md, vsel=vsel, Eg=Eg, Ed=Ed, runs=runs)


def make_inmaps(C, HS, x_full, wqkv, wout, wq, wk, wv, wo, bq, bk, bv, cps):
    E, F, TOK, D, H = C["E"], C["F"], C["TOK"], C["D"], C["H"]
    NHG, GROWS, NORM0, GOROWS, S = (C["NHG"], C["GROWS"], C["NORM0"],
                                    C["GOROWS"], C["S"])
    RT, NCTRL, TCH = C["RT"], C["NCTRL"], C["TCH"]
    g = HS["grid"]
    K1 = 128 // F
    hh = max(1, 128 // C["ITILES"])
    NQ = max(1, C["ITILES"] // 16)
    bias3 = np.zeros((128, 3 * F), dtype=np.float32)
    for w, b in enumerate([bq, bk, bv]):
        bias3[:, w * F:(w + 1) * F] = np.asarray(b, np.float32).reshape(F, 128).T
    cpads = np.stack([np.pad(np.asarray(c, np.float32), (2, 6), mode="edge")
                      for c in cps])
    phii = np.concatenate([g["phi_a"], g["phi_b"]], axis=1)
    dmaski = g["dmask"].astype(np.uint8)
    ident = np.eye(128, dtype=np.float32)
    ones = np.ones((128, 132), dtype=np.float32)
    ones[:, 130] = 1e-5
    ones[:, 131] = 0.0
    common = dict(
        wqkv=np.ascontiguousarray(np.asarray(wqkv, np.float32)),
        wout=np.ascontiguousarray(np.asarray(wout, np.float32)),
        wq=np.ascontiguousarray(np.asarray(wq, np.float32)),
        wk=np.ascontiguousarray(np.asarray(wk, np.float32)),
        wv=np.ascontiguousarray(np.asarray(wv, np.float32)),
        wo=np.ascontiguousarray(np.asarray(wo, np.float32)),
        biases=bias3, cpads=cpads,
        maskg=HS["maskg"].reshape(128, -1), maskd=HS["maskd"].reshape(128, -1),
        vseli=HS["vsel"],
        egi=HS["Eg"].reshape(K1 * 8 * 128, 128),
        edi=HS["Ed"].reshape(hh * NQ * 8 * 128, 128),
        identfi=ident, identbi=ident.astype(BF),
        onesi=ones, rmti=rot_matrix(),
        dmti=diag_masks(C), phii=phii, dmaski=dmaski,
    )
    in_maps = []
    for c in range(NCORES):
        b, qq = c // 4, c % 4
        pos = qq * TOK + np.arange(TOK)
        cosv, sinv = rope_tables(C, pos)
        coff = np.array([[(4 * b * GROWS + NHG * qq * 3 * D) * TCH,
                          (4 * b * GROWS + NORM0 + 2 * NHG * qq) * TCH,
                          4 * b * D * S + qq * TOK, 0]], dtype=np.uint32)
        xin = np.ascontiguousarray(
            np.asarray(x_full[b, qq * TOK:(qq + 1) * TOK, :], np.float32).T)
        m = dict(common)
        m.update(cosi=cosv, sini=sinv, coffi=coff, xin=xin)
        in_maps.append(m)
    return in_maps


_CACHE = {}


def kernel(L0_attn_a_x, L0_attn_a_Wqkv, L0_attn_a_Wout,
           L0_attn_b_Wq, L0_attn_b_bq, L0_attn_b_Wk, L0_attn_b_bk,
           L0_attn_b_Wv, L0_attn_b_bv, L0_attn_b_Wo,
           L0_ffn_gate_cp, L0_ffn_up_cp, L0_ffn_down_cp):
    C = make_cfg(True)
    if "nc" not in _CACHE:
        HS = host_static(C)
        _CACHE["HS"] = HS
        _CACHE["nc"] = build_program(C, HS)
    HS = _CACHE["HS"]
    nc = _CACHE["nc"]
    in_maps = make_inmaps(C, HS, np.asarray(L0_attn_a_x),
                          L0_attn_a_Wqkv, L0_attn_a_Wout,
                          L0_attn_b_Wq, L0_attn_b_Wk, L0_attn_b_Wv, L0_attn_b_Wo,
                          L0_attn_b_bq, L0_attn_b_bk, L0_attn_b_bv,
                          [L0_ffn_gate_cp, L0_ffn_up_cp, L0_ffn_down_cp])
    res = run_bass_kernel_spmd(nc, in_maps, core_ids=list(range(NCORES)))
    out = np.zeros((C["B"], C["S"], C["E"]), dtype=np.float32)
    TOK = C["TOK"]
    for c in range(NCORES):
        b, qq = c // 4, c % 4
        out[b, qq * TOK:(qq + 1) * TOK, :] = res.results[c]["xout"].T
    return out


# version 15
# speedup vs baseline: 1.0919x; 1.0919x over previous
import math
import numpy as np
import ml_dtypes
import concourse.bass as bass
import concourse.mybir as mybir
from concourse.ap import AP
from concourse.bass_utils import run_bass_kernel_spmd
from contextlib import ExitStack

F32 = mybir.dt.float32
BF16 = mybir.dt.bfloat16
U32 = mybir.dt.uint32
U8 = mybir.dt.uint8
AF = mybir.ActivationFunctionType
ALU = mybir.AluOpType
BF = ml_dtypes.bfloat16
SP = mybir.EngineType.SP

NCORES = 8


def make_cfg(full=True):
    c = {}
    if full:
        c.update(B=2, S=8192, E=2048, H=16, D=128, INTER=8192, COMPRESSION=128.9)
    else:
        c.update(B=2, S=2048, E=512, H=4, D=128, INTER=1024, COMPRESSION=128.9)
    c["TOK"] = c["B"] * c["S"] // NCORES
    c["TCH"] = min(512, c["TOK"] // 2)
    c["NT"] = c["TOK"] // c["TCH"]
    c["F"] = c["E"] // 128
    c["NHG"] = 4 * c["H"] // 16
    assert c["NHG"] * 8 == c["B"] * c["H"]
    c["QB"] = c["S"] // 8
    c["NQB"] = 8
    c["QT"] = c["QB"] // 128
    c["KC"] = min(512, c["QB"])
    c["NKC"] = c["S"] // c["KC"]
    c["GROWS"] = c["H"] * 3 * c["D"] + 2 * c["H"]
    c["NORM0"] = c["H"] * 3 * c["D"]
    c["GOROWS"] = c["NHG"] * c["D"]
    c["NSAMP"] = c["E"] * c["INTER"]
    c["NCTRL"] = max(16, int(c["NSAMP"] / c["COMPRESSION"]))
    c["R"] = c["NSAMP"] // 128
    c["RT"] = c["R"] // 128
    c["ITILES"] = c["INTER"] // 128
    c["SM_SCALE"] = 1.0 / math.sqrt(c["D"])
    return c



def spline_grid(C):
    n = C["NSAMP"]
    nct = C["NCTRL"]
    t = np.linspace(0.0, nct - 1.0, n, dtype=np.float64)
    i = np.clip(np.floor(t).astype(np.int64), 0, nct - 2)
    R = C["R"]
    RT = C["RT"]
    r = np.arange(R)
    I_r = i[r * 128]
    i_rows = i.reshape(R, 128)
    assert (i_rows.max(axis=1) - I_r <= 1).all() and (i_rows.min(axis=1) == I_r).all()
    phi = (t[r * 128] - I_r)
    base_T = I_r.reshape(RT, 128)[:, 0]
    Ir2 = I_r.reshape(RT, 128)
    d = (base_T[:, None] + np.arange(128)[None, :]) - Ir2
    assert d.min() >= 0 and d.max() <= 1, (d.min(), d.max())
    dmask = d.T.astype(np.float32).copy()
    phi_a = phi.reshape(RT, 128).T.astype(np.float32).copy()
    phi_b = (phi - 1.0).reshape(RT, 128).T.astype(np.float32).copy()
    scount = (i_rows == I_r[:, None]).sum(axis=1)
    dt = (nct - 1.0) / (n - 1.0)
    return dict(I_r=I_r, base_T=base_T, dmask=dmask, phi_a=phi_a, phi_b=phi_b,
                scount=scount, dt=dt)


def spline_masks(C, g):
    R = C["R"]
    sc = g["scount"]
    s = np.arange(128)
    mrow = (s[None, :] >= sc[:, None])
    F = C["F"]
    mg = mrow.reshape(C["INTER"], F, 128).transpose(2, 1, 0)
    IT = C["ITILES"]
    md = mrow.reshape(C["E"], IT, 128).transpose(2, 1, 0)
    return (np.ascontiguousarray(mg.astype(np.uint8)),
            np.ascontiguousarray(md.astype(np.uint8)))


def spline_vsel(C, g):
    dt = g["dt"]
    s = np.arange(128, dtype=np.float64)
    pw = np.stack([(s * dt) ** m for m in range(4)])
    v = np.zeros((128, 8 * 128), dtype=np.float64)
    for gg in range(4):
        for hp in range(4):
            for ev in range(2):
                for m in range(4):
                    v[32 * gg + 8 * hp + 4 * ev + m,
                      (hp * 2 + ev) * 128:(hp * 2 + ev + 1) * 128] = pw[m]
    return np.ascontiguousarray(v.astype(BF))


def spline_eperm(C):
    F = C["F"]
    K1 = 128 // F
    Eg = np.zeros((K1, 8, 128, 128), dtype=np.float64)
    for il in range(K1):
        for m in range(8):
            for h16 in range(F):
                Eg[il, m, F * il + h16, 8 * h16 + m] = 1.0
    IT = C["ITILES"]
    hh = 128 // IT
    NQ = max(1, IT // 16)
    Ed = np.zeros((hh, NQ, 8, 128, 128), dtype=np.float64)
    for eps in range(hh):
        for q in range(NQ):
            for m in range(8):
                for ip in range(min(16, IT)):
                    i16 = 16 * q + ip
                    if i16 < IT:
                        Ed[eps, q, m, IT * eps + i16, 8 * ip + m] = 1.0
    return (np.ascontiguousarray(Eg.astype(BF)),
            np.ascontiguousarray(Ed.astype(BF)))


def mload_runs(C, g):
    base = g["base_T"]
    RT = C["RT"]
    runs = []
    t0 = 0
    while t0 < RT:
        t1 = t0 + 1
        step = None
        while t1 < RT and (t1 % 128 != 0):
            st = int(base[t1] - base[t1 - 1])
            if step is None:
                step = st
            elif st != step:
                break
            t1 += 1
        runs.append((t0, t1 - t0, int(step if step is not None else 0)))
        t0 = t1
    return runs


def rope_tables(C, positions):
    D = C["D"]
    inv = 1.0 / (10000.0 ** (np.arange(0, D, 2, dtype=np.float64) / D))
    j = np.arange(D)
    invj = inv[j % (D // 2)]
    ang = positions[None, :].astype(np.float64) * invj[:, None]
    return np.cos(ang).astype(np.float32), np.sin(ang).astype(np.float32)


def rot_matrix():
    Rm = np.zeros((128, 128), dtype=np.float64)
    for i in range(64):
        Rm[2 * i, 2 * i + 1] = -1.0
        Rm[2 * i + 1, 2 * i] = 1.0
    return np.ascontiguousarray(Rm.T.astype(BF))


def diag_masks(C):
    m = np.zeros((128, 4 * 512), dtype=np.float32)
    for o in range(4):
        kk = np.arange(128)[:, None]
        qc = np.arange(512)[None, :]
        m[:, o * 512:(o + 1) * 512] = np.where(o * 128 + kk > qc, -1e30, 0.0)
    return m.astype(BF)



class Ctr:
    def __init__(self, es, nc, name):
        self.h = es.enter_context(nc.semaphore(name))
        self.n = 0

    def inc(self, instr, k=1):
        instr.then_inc(self.h, k)
        self.n += k

    def dinc(self, instr):
        instr.then_inc(self.h, 16)
        self.n += 16


def sbap(handle, rowlen, p0, coff, dims):
    pcount = dims[0]
    free = dims[1:]
    return AP(handle, p0 * rowlen + coff, [[rowlen, pcount]] + [list(x) for x in free])


def build_program(C, HOST):
    grid = HOST["grid"]
    nc = bass.Bass("TRN2", target_bir_lowering=False)
    E, F, TOK, TCH, NT = C["E"], C["F"], C["TOK"], C["TCH"], C["NT"]
    H, D, S, QB, KC, QT = C["H"], C["D"], C["S"], C["QB"], C["KC"], C["QT"]
    NHG, GROWS, NORM0, GOROWS = C["NHG"], C["GROWS"], C["NORM0"], C["GOROWS"]
    INTER, ITILES, R, RT, NCTRL = C["INTER"], C["ITILES"], C["R"], C["RT"], C["NCTRL"]
    SUB = min(512, TCH)
    NSUB = TCH // SUB
    K1 = 128 // F
    hh = 128 // ITILES if ITILES < 128 else 1
    hh = max(1, 128 // ITILES)
    NQ = max(1, ITILES // 16)
    NOFF = KC // 128

    dram = lambda n, sh, dt: nc.dram_tensor(n, sh, dt)

    xin = nc.dram_tensor("xin", [E, TOK], F32, kind="ExternalInput")
    wqkv = nc.dram_tensor("wqkv", [3 * E, E], F32, kind="ExternalInput")
    wout = nc.dram_tensor("wout", [E, E], F32, kind="ExternalInput")
    wq = nc.dram_tensor("wq", [E, E], F32, kind="ExternalInput")
    wk = nc.dram_tensor("wk", [E, E], F32, kind="ExternalInput")
    wv = nc.dram_tensor("wv", [E, E], F32, kind="ExternalInput")
    wo = nc.dram_tensor("wo", [E, E], F32, kind="ExternalInput")
    biases = nc.dram_tensor("biases", [128, 3 * F], F32, kind="ExternalInput")
    cpads = nc.dram_tensor("cpads", [3, NCTRL + 8], F32, kind="ExternalInput")
    maskg = nc.dram_tensor("maskg", [128, F * INTER], U8, kind="ExternalInput")
    maskd = nc.dram_tensor("maskd", [128, ITILES * E], U8, kind="ExternalInput")
    vseli = nc.dram_tensor("vseli", [128, 1024], BF16, kind="ExternalInput")
    egi = nc.dram_tensor("egi", [K1 * 8 * 128, 128], BF16, kind="ExternalInput")
    edi = nc.dram_tensor("edi", [hh * NQ * 8 * 128, 128], BF16, kind="ExternalInput")
    identfi = nc.dram_tensor("identfi", [128, 128], F32, kind="ExternalInput")
    identbi = nc.dram_tensor("identbi", [128, 128], BF16, kind="ExternalInput")
    onesi = nc.dram_tensor("onesi", [128, 132], F32, kind="ExternalInput")
    rmti = nc.dram_tensor("rmti", [128, 128], BF16, kind="ExternalInput")
    cosi = nc.dram_tensor("cosi", [128, TOK], F32, kind="ExternalInput")
    sini = nc.dram_tensor("sini", [128, TOK], F32, kind="ExternalInput")
    dmti = nc.dram_tensor("dmti", [128, 4 * 512], BF16, kind="ExternalInput")
    coffi = nc.dram_tensor("coffi", [1, 4], U32, kind="ExternalInput")
    phii = nc.dram_tensor("phii", [128, 2 * RT], F32, kind="ExternalInput")
    dmaski = nc.dram_tensor("dmaski", [128, RT], U8, kind="ExternalInput")
    xout = nc.dram_tensor("xout", [E, TOK], F32, kind="ExternalOutput")
    import os
    KDBG = bool(os.environ.get("KDBG"))
    if KDBG:
        dbg1 = nc.dram_tensor("dbg1", [E, TOK], F32, kind="ExternalOutput")
        dbg2 = nc.dram_tensor("dbg2", [E, TOK], F32, kind="ExternalOutput")
        dbg3 = nc.dram_tensor("dbg3", [E, TOK], F32, kind="ExternalOutput")

    x1 = dram("x1", [E, TOK], F32)
    x2 = dram("x2", [E, TOK], F32)
    x3 = dram("x3", [E, TOK], F32)
    wqkvT = dram("wqkvT", [F, 128, 3 * E], BF16)
    woutT = dram("woutT", [F, 128, E], BF16)
    gwT = dram("gwT", [F, 128, INTER], BF16)
    uwT = dram("uwT", [F, 128, INTER], BF16)
    dwT = dram("dwT", [ITILES, 128, E], BF16)
    gin = dram("gin", [NT * GROWS, TCH], BF16)
    G = nc.dram_tensor("G", [NT * 8 * GROWS, TCH], BF16, addr_space="Shared")
    goin = dram("goin", [GOROWS, S], BF16)
    GO = nc.dram_tensor("GO", [NHG * 8 * D, S], BF16, addr_space="Shared")
    GL = dram("GL", [4 * NHG * 3 * D, TOK], BF16)
    NL = dram("NL", [8 * NHG, TOK], BF16)
    AOL = dram("AOL", [E, TOK], BF16)

    es = ExitStack()
    _uid = [0]

    def _nm(n):
        _uid[0] += 1
        return f"{n}_{_uid[0]}"

    ident_f = es.enter_context(nc.sbuf_tensor(_nm("ident_f"), [128, 128], F32))
    ident_b = es.enter_context(nc.sbuf_tensor(_nm("ident_b"), [128, 128], BF16))
    ones_f = es.enter_context(nc.sbuf_tensor(_nm("ones_f"), [128, 132], F32))
    nc.const_aps.register(F32, 1e-5, ones_f[:, 130:131])
    nc.const_aps.register(F32, 1.0, ones_f[:, 0:1])
    nc.const_aps.register(F32, 0.0, ones_f[:, 131:132])
    ones_b = es.enter_context(nc.sbuf_tensor(_nm("ones_b"), [128, 2], BF16))
    rmt_sb = es.enter_context(nc.sbuf_tensor(_nm("rmt_sb"), [128, 128], BF16))
    cos_sb = es.enter_context(nc.sbuf_tensor(_nm("cos_sb"), [128, TOK], F32))
    sin_sb = es.enter_context(nc.sbuf_tensor(_nm("sin_sb"), [128, TOK], F32))
    dmt_sb = es.enter_context(nc.sbuf_tensor(_nm("dmt_sb"), [128, 4 * 512], BF16))
    bias_sb = es.enter_context(nc.sbuf_tensor(_nm("bias_sb"), [128, 3 * F], F32))
    coff_sb = es.enter_context(nc.sbuf_tensor(_nm("coff_sb"), [1, 4], U32))
    sumx_sb = es.enter_context(nc.sbuf_tensor(_nm("sumx_sb"), [1, TOK], F32))
    sumx2_sb = es.enter_context(nc.sbuf_tensor(_nm("sumx2_sb"), [1, TOK], F32))

    sems = {}
    for name in ["sd", "w0", "w1", "x0", "x1", "s0", "s1", "r0", "r1",
                 "o0", "o1", "g0", "g1", "vd", "kd", "ad", "gd", "cc", "pe",
                 "pw", "po", "se", "ve", "vf", "fi", "fo", "so", "sp", "od"]:
        sems[name] = Ctr(es, nc, name)
    sd, kd, ad, gd, cc = (sems[k] for k in ["sd", "kd", "ad", "gd", "cc"])
    pe, pw, se, ve, vf, sp, od = (sems[k] for k in ["pe", "pw", "se", "ve", "vf", "sp", "od"])

    regs = {}

    with nc.Block() as blk:
        @blk.sync
        def _(sy):
            for dst, src in [(ident_f, identfi), (ident_b, identbi), (ones_f, onesi),
                             (rmt_sb, rmti), (cos_sb, cosi), (sin_sb, sini),
                             (dmt_sb, dmti), (bias_sb, biases), (coff_sb, coffi)]:
                sd.dinc(sy.dma_start(out=dst[:, :], in_=src.ap()[:, :]))
            sy.wait_ge(sd.h, sd.n)
            regs["g"] = nc.values_load(coff_sb[0:1, 0:1], engines=[SP],
                                       min_val=0, max_val=8 * GROWS * TOK,
                                       skip_runtime_bounds_check=True)
            regs["nq"] = nc.values_load(coff_sb[0:1, 1:2], engines=[SP],
                                        min_val=0, max_val=8 * GROWS * TOK,
                                        skip_runtime_bounds_check=True)
            regs["go"] = nc.values_load(coff_sb[0:1, 2:3], engines=[SP],
                                        min_val=0, max_val=8 * GOROWS * S,
                                        skip_runtime_bounds_check=True)

        @blk.gpsimd
        def _(g):
            g.wait_ge(sd.h, sd.n)
            gd.dinc(g.dma_start(out=ones_b[:, :], in_=onesi.ap()[:, 0:2]))
            g.wait_ge(gd.h, gd.n)

    def ln_rows_s1(sc):
        sc.mul(mrow_sb[0:1, :], sumx_sb[0:1, :], 1.0 / E)
        sc.drain()
        sc.square(trow_sb[0:1, :], mrow_sb[0:1, :])
        se.inc(sc.mul(rrow_sb[0:1, :], sumx2_sb[0:1, :], 1.0 / E))

    def ln_rows_v(v):
        ve.inc(v.tensor_tensor(trow_sb[0:1, :], rrow_sb[0:1, :], trow_sb[0:1, :],
                               ALU.subtract))

    def ln_rows_s2(sc):
        sc.activation(trow_sb[0:1, :], trow_sb[0:1, :], AF.Ln, bias=1e-5)
        sc.drain()
        se.inc(sc.activation(rrow_sb[0:1, :], trow_sb[0:1, :], AF.Exp, scale=-0.5))

    SS = 512 if TOK >= 512 else TOK
    NS = TOK // SS

    with nc.Block() as blk:
        p0s = ExitStack()
        xt = p0s.enter_context(nc.sbuf_tensor(_nm("p0_xt"), [128, TOK], F32))
        xb = p0s.enter_context(nc.sbuf_tensor(_nm("p0_xb"), [128, TOK], BF16))
        x2b = p0s.enter_context(nc.sbuf_tensor(_nm("p0_x2b"), [128, TOK], BF16))
        psx = [p0s.enter_context(nc.psum_tensor(_nm(f"p0_psx{i}"), [1, SS], F32))
               for i in range(NS)]
        psx2 = [p0s.enter_context(nc.psum_tensor(_nm(f"p0_psx2{i}"), [1, SS], F32))
                for i in range(NS)]
        sd0, se0, pe0 = sd.n, se.n, pe.n

        @blk.sync
        def _(sy):
            for f in range(F):
                if f >= 1:
                    sy.wait_ge(se.h, se0 + 2 * f)
                sd.dinc(sy.dma_start(out=xt[:, :], in_=xin.ap()[f * 128:(f + 1) * 128, :]))

        @blk.scalar
        def _(sc):
            for f in range(F):
                sc.wait_ge(sd.h, sd0 + 16 * (f + 1))
                if f >= 1:
                    sc.wait_ge(pe.h, pe0 + f)
                se.inc(sc.copy(xb[:, :], xt[:, :]))
                se.inc(sc.square(x2b[:, :], xt[:, :]))

        @blk.tensor
        def _(t):
            for f in range(F):
                t.wait_ge(se.h, se0 + 2 * (f + 1))
                for s_ in range(NS):
                    t.matmul(psx[s_][0:1, :], ones_b[:, 0:1], xb[:, s_ * SS:(s_ + 1) * SS],
                             start=(f == 0), stop=(f == F - 1))
                    last = t.matmul(psx2[s_][0:1, :], ones_b[:, 0:1],
                                    x2b[:, s_ * SS:(s_ + 1) * SS],
                                    start=(f == 0), stop=(f == F - 1))
                pe.inc(last)

        @blk.vector
        def _(v):
            v.wait_ge(pe.h, pe0 + F)
            for s_ in range(NS):
                v.tensor_copy(sumx_sb[0:1, s_ * SS:(s_ + 1) * SS], psx[s_][0:1, :])
                last = v.tensor_copy(sumx2_sb[0:1, s_ * SS:(s_ + 1) * SS], psx2[s_][0:1, :])
            ve.inc(last)
        p0s.close()

    for wname, wsrc, wdst, OC in [("wqkv", wqkv, wqkvT, 3 * E // 128),
                                  ("wout", wout, woutT, E // 128)]:
        with nc.Block() as blk:
            ps_ = ExitStack()
            wtmp = [ps_.enter_context(nc.sbuf_tensor(_nm(f"{wname}_t{i}"), [128, 128], F32))
                    for i in range(2)]
            wbf = [ps_.enter_context(nc.sbuf_tensor(_nm(f"{wname}_b{i}"), [128, 128], BF16))
                   for i in range(2)]
            pst = [ps_.enter_context(nc.psum_tensor(_nm(f"{wname}_p{i}"), [128, 128], BF16))
                   for i in range(2)]
            rowb = ps_.enter_context(nc.sbuf_tensor(_nm(f"{wname}_row"), [128, OC * 128], BF16))
            se0, pe0, ve0, ad0 = se.n, pe.n, ve.n, ad.n
            NTL = F * OC
            wsem = [sems["w0"], sems["w1"]]
            wval = {}

            def idx(f, oc):
                return f * OC + oc

            @blk.sync
            def _(sy):
                for f in range(F):
                    for oc in range(OC):
                        i = idx(f, oc)
                        if i >= 2:
                            sy.wait_ge(ve.h, ve0 + i - 1)
                        src = AP(wsrc, (oc * 128) * E + f * 128, [[E, 128], [1, 128]])
                        wsem[i % 2].dinc(sy.dma_start(out=wtmp[i % 2][:, :], in_=src))
                        wval[i] = wsem[i % 2].n

            @blk.vector
            def _(v):
                for f in range(F):
                    for oc in range(OC):
                        i = idx(f, oc)
                        v.wait_ge(wsem[i % 2].h, wval[i])
                        if i >= 2:
                            v.wait_ge(pe.h, pe0 + i - 1)
                        ve.inc(v.tensor_copy(wbf[i % 2][:, :], wtmp[i % 2][:, :]))

            @blk.tensor
            def _(t):
                for f in range(F):
                    for oc in range(OC):
                        i = idx(f, oc)
                        t.wait_ge(ve.h, ve0 + i + 1)
                        if i >= 2:
                            t.wait_ge(se.h, se0 + i - 1)
                        pe.inc(t.transpose(pst[i % 2][:, :], wbf[i % 2][:, :], ident_b[:, :]))

            @blk.scalar
            def _(sc):
                for f in range(F):
                    sc.wait_ge(ad.h, ad0 + 16 * f)
                    for oc in range(OC):
                        i = idx(f, oc)
                        sc.wait_ge(pe.h, pe0 + i + 1)
                        se.inc(sc.copy(rowb[:, oc * 128:(oc + 1) * 128], pst[i % 2][:, :]))

            @blk.gpsimd
            def _(g):
                for f in range(F):
                    g.wait_ge(se.h, se0 + (f + 1) * OC)
                    ad.dinc(g.dma_start(out=AP(wdst, f * 128 * OC * 128,
                                               [[OC * 128, 128], [1, OC * 128]]),
                                        in_=rowb[:, :]))
                g.wait_ge(ad.h, ad.n)
            ps_.close()

    MW = 136
    NCH = RT // 128 if RT >= 128 else 1
    CHT = min(128, RT)
    RH = min(512, RT)
    NRH = RT // RH

    def p0d(widx, runs):
        ssp = ExitStack()
        pk = ssp.enter_context(nc.sbuf_tensor(_nm("sp_pk"), [128, 5 * RT], F32))
        ca = ssp.enter_context(nc.sbuf_tensor(_nm("sp_ca"), [128, 8 * RT], F32))
        psz = ssp.enter_context(nc.psum_tensor(_nm("sp_psz"), [128, RH], F32))

        def P(k):
            return pk[:, k * RT:(k + 1) * RT]

        def CA(m):
            return ca[:, m * RT:(m + 1) * RT]

        inner = ExitStack()
        m_sb = [inner.enter_context(nc.sbuf_tensor(_nm("sp_m"), [CHT, MW], F32))
                for i in range(2)]
        w7 = inner.enter_context(nc.sbuf_tensor(_nm("sp_w7"), [128, 6 * RT], F32))
        psm = [inner.enter_context(nc.psum_tensor(_nm("sp_psm"), [128, 128], F32))
               for i in range(2)]
        se0, pe0, ve0 = se.n, pe.n, ve.n
        msem = [sems["x0"], sems["x1"]]
        chval = {}
        with nc.Block() as blk:
            @blk.sync
            def _(sy):
                for ch in range(NCH):
                    if ch >= 2:
                        sy.wait_ge(pe.h, pe0 + 6 * (ch - 1))
                    for (t0, ln, step) in runs:
                        if not (ch * 128 <= t0 < (ch + 1) * 128):
                            continue
                        src = AP(cpads, widx * (NCTRL + 8) + int(grid["base_T"][t0]),
                                 [[step, ln], [1, MW - 2]])
                        dst = m_sb[ch % 2][t0 - ch * 128: t0 - ch * 128 + ln, 0:MW - 2]
                        msem[ch % 2].dinc(sy.dma_start(out=dst, in_=src))
                    chval[ch] = msem[ch % 2].n

            @blk.tensor
            def _(t):
                for ch in range(NCH):
                    t.wait_ge(msem[ch % 2].h, chval[ch])
                    for c in range(6):
                        j = 6 * ch + c
                        if j >= 2:
                            t.wait_ge(se.h, se0 + j - 1)
                        pe.inc(t.transpose(psm[c % 2][:, 0:CHT],
                                           m_sb[ch % 2][:, c:c + 128],
                                           ident_f[0:CHT, 0:CHT]))

            @blk.scalar
            def _(sc):
                for ch in range(NCH):
                    for c in range(6):
                        sc.wait_ge(pe.h, pe0 + ch * 6 + c + 1)
                        se.inc(sc.copy(w7[:, c * RT + ch * CHT: c * RT + ch * CHT + CHT],
                                       psm[c % 2][:, 0:CHT]))

            @blk.vector
            def _(v):
                v.wait_ge(se.h, se0 + 6 * NCH)
                for k in range(5):
                    v.tensor_copy(P(k), w7[:, (k + 1) * RT:(k + 2) * RT])
                    v.drain()
                    ve.inc(v.copy_predicated(P(k), dmask_sb[:, :],
                                             w7[:, k * RT:(k + 1) * RT]))
                    v.drain()
        inner.close()

        inner = ExitStack()
        tmp = inner.enter_context(nc.sbuf_tensor(_nm("sp_tmp"), [128, 5 * RT], F32))
        with nc.Block() as blk:
            @blk.vector
            def _(v):
                def T_(i):
                    return tmp[:, i * RT:(i + 1) * RT]

                def vop(op):
                    v.drain()
                for e in range(2):
                    phi = php_sb[:, e * RT:(e + 1) * RT]
                    p0_, p1_, p2_, p3_ = P(e), P(e + 1), P(e + 2), P(e + 3)
                    A1, A2, A3, u1, u2 = T_(0), T_(1), T_(2), T_(3), T_(4)
                    vop(v.tensor_tensor(u1, p2_, p0_, ALU.subtract))
                    vop(v.tensor_scalar(A1, u1, 0.5, None, ALU.mult))
                    vop(v.scalar_tensor_tensor(u2, p1_, -2.5, p0_, ALU.mult, ALU.add))
                    vop(v.scalar_tensor_tensor(u1, p3_, -0.5, u2, ALU.mult, ALU.add))
                    vop(v.scalar_tensor_tensor(A2, p2_, 2.0, u1, ALU.mult, ALU.add))
                    vop(v.tensor_tensor(u1, p3_, p0_, ALU.subtract))
                    vop(v.tensor_tensor(u2, p1_, p2_, ALU.subtract))
                    vop(v.tensor_scalar(u1, u1, 0.5, None, ALU.mult))
                    vop(v.scalar_tensor_tensor(A3, u2, 1.5, u1, ALU.mult, ALU.add))
                    vop(v.tensor_copy(CA(e * 4 + 3), A3))
                    vop(v.tensor_tensor(u1, A3, phi, ALU.mult))
                    vop(v.scalar_tensor_tensor(CA(e * 4 + 2), u1, 3.0, A2,
                                               ALU.mult, ALU.add))
                    vop(v.tensor_tensor(u2, A2, CA(e * 4 + 2), ALU.add))
                    vop(v.tensor_tensor(u1, u2, phi, ALU.mult))
                    vop(v.tensor_tensor(CA(e * 4 + 1), u1, A1, ALU.add))
                    vop(v.tensor_tensor(u1, A3, phi, ALU.mult))
                    vop(v.tensor_tensor(u2, u1, A2, ALU.add))
                    vop(v.tensor_tensor(u1, u2, phi, ALU.mult))
                    vop(v.tensor_tensor(u2, u1, A1, ALU.add))
                    vop(v.tensor_tensor(u1, u2, phi, ALU.mult))
                    vop(v.tensor_tensor(CA(e * 4 + 0), u1, p1_, ALU.add))
        inner.close()

        inner = ExitStack()
        cab = inner.enter_context(nc.sbuf_tensor(_nm("sp_cab"), [128, 8 * RT], BF16))
        se0, pe0 = se.n, pe.n
        with nc.Block() as blk:
            @blk.scalar
            def _(sc):
                for m in range(8):
                    se.inc(sc.copy(cab[:, m * RT:(m + 1) * RT], CA(m)))
                nz = 0
                if widx < 2:
                    F8 = F8cur["t"]
                    for il in range(K1):
                        for half in range(NRH):
                            sc.wait_ge(pe.h, pe0 + nz + 1)
                            se.inc(sc.copy(
                                sbap(F8, INTER, 0, half * RH * K1 + il, [128, (K1, RH)]),
                                psz[:, :]))
                            nz += 1
                else:
                    for eps in range(hh):
                        for q in range(NQ):
                            for half in range(NRH):
                                sc.wait_ge(pe.h, pe0 + nz + 1)
                                se.inc(sc.copy(
                                    sbap(F8cur["t"], NQ * E, 0,
                                         q * E + eps + half * RH * hh, [128, (hh, RH)]),
                                    psz[:, :]))
                                nz += 1

            @blk.tensor
            def _(t):
                t.wait_ge(se.h, se0 + 8)
                nz = 0
                if widx < 2:
                    for il in range(K1):
                        for half in range(NRH):
                            for m in range(8):
                                mm = t.matmul(psz[:, :],
                                              eg_sb[:, (il * 8 + m) * 128:(il * 8 + m + 1) * 128],
                                              cab[:, m * RT + half * RH: m * RT + half * RH + RH],
                                              start=(m == 0), stop=(m == 7))
                            pe.inc(mm)
                            nz += 1
                            t.wait_ge(se.h, se0 + 8 + nz)
                else:
                    for eps in range(hh):
                        for q in range(NQ):
                            for half in range(NRH):
                                for m in range(8):
                                    mm = t.matmul(psz[:, :],
                                                  ed_sb[:, ((eps * NQ + q) * 8 + m) * 128:
                                                        ((eps * NQ + q) * 8 + m + 1) * 128],
                                                  cab[:, m * RT + half * RH: m * RT + half * RH + RH],
                                                  start=(m == 0), stop=(m == 7))
                                pe.inc(mm)
                                nz += 1
                                t.wait_ge(se.h, se0 + 8 + nz)
        inner.close()
        ssp.close()

    spl = ExitStack()
    php_sb = spl.enter_context(nc.sbuf_tensor(_nm("php_sb"), [128, 2 * RT], F32))
    dmask_sb = spl.enter_context(nc.sbuf_tensor(_nm("dmask_sb"), [128, RT], U8))
    eg_sb = spl.enter_context(nc.sbuf_tensor(_nm("eg_sb"), [128, K1 * 8 * 128], BF16))
    ed_sb = spl.enter_context(nc.sbuf_tensor(_nm("ed_sb"), [128, hh * NQ * 8 * 128], BF16))
    vsel_sb = spl.enter_context(nc.sbuf_tensor(_nm("vsel_sb"), [128, 1024], BF16))
    F8cur = {}

    with nc.Block() as blk:
        @blk.sync
        def _(sy):
            sd.dinc(sy.dma_start(out=php_sb[:, :], in_=phii.ap()[:, :]))
            sd.dinc(sy.dma_start(out=dmask_sb[:, :], in_=dmaski.ap()[:, :]))
            sd.dinc(sy.dma_start(out=vsel_sb[:, :], in_=vseli.ap()[:, :]))
            NEG = K1 * 8
            sd.dinc(sy.dma_start(
                out=AP(eg_sb, 0, [[NEG * 128, 128], [128, NEG], [1, 128]]),
                in_=AP(egi, 0, [[128, 128], [128 * 128, NEG], [1, 128]])))
            NED = hh * NQ * 8
            sd.dinc(sy.dma_start(
                out=AP(ed_sb, 0, [[NED * 128, 128], [128, NED], [1, 128]]),
                in_=AP(edi, 0, [[128, 128], [128 * 128, NED], [1, 128]])))
            sy.wait_ge(sd.h, sd.n)

    HOSTR = HOST["runs"]

    def p0e(widx):
        F8 = F8cur["t"]
        wdst = [gwT, uwT, dwT][widx]
        mask = maskg if widx < 2 else maskd
        NTO = F if widx < 2 else ITILES
        NIB = (INTER if widx < 2 else E) // 512
        OUTC = INTER if widx < 2 else E
        ssp = ExitStack()
        msk = [ssp.enter_context(nc.sbuf_tensor(_nm(f"ev_m{i}"), [128, 512], U8))
               for i in range(4)]
        wtb = [ssp.enter_context(nc.sbuf_tensor(_nm(f"ev_w{i}"), [128, 512], BF16))
               for i in range(4)]
        psa = [ssp.enter_context(nc.psum_tensor(_nm(f"ev_pa{i}"), [128, 512], F32))
               for i in range(2)]
        psb = [ssp.enter_context(nc.psum_tensor(_nm(f"ev_pb{i}"), [128, 512], F32))
               for i in range(2)]
        se0, pe0, ve0 = se.n, pe.n, ve.n
        xsem = [sems["x0"], sems["x1"], sems["r0"], sems["r1"]]
        osem = [sems["g0"], sems["g1"]]
        xval, oval = {}, {}
        NTOT = NTO * NIB
        ob = [osem[0].n, osem[1].n]
        for i in range(NTOT):
            oval[i] = ob[i % 2] + 16 * (i // 2 + 1)
        with nc.Block() as blk:
            @blk.sync
            def _(sy):
                i = 0
                for to in range(NTO):
                    for ib in range(NIB):
                        if i >= 4:
                            sy.wait_ge(ve.h, ve0 + i - 3)
                        xsem[i % 4].dinc(sy.dma_start(
                            out=msk[i % 4][:, :],
                            in_=AP(mask, to * OUTC + ib * 512,
                                   [[NTO * OUTC, 128], [1, 512]])))
                        xval[i] = xsem[i % 4].n
                        i += 1

            @blk.tensor
            def _(t):
                i = 0
                for to in range(NTO):
                    gg = (to % 16) // 4
                    hp = to % 4
                    tp = (32 * gg, 0)
                    qq = to // 16
                    for ib in range(NIB):
                        if i >= 2:
                            t.wait_ge(se.h, se0 + i - 1)
                            t.wait_ge(ve.h, ve0 + i - 1)
                        if widx < 2:
                            rhs_a = F8[32 * gg:32 * gg + 32, ib * 512:(ib + 1) * 512]
                            rhs_b = rhs_a
                        else:
                            rhs_a = F8[32 * gg:32 * gg + 32,
                                       qq * E + ib * 512: qq * E + (ib + 1) * 512]
                            rhs_b = rhs_a
                        pe.inc(t.matmul(psa[i % 2][:, :],
                                        vsel_sb[32 * gg:32 * gg + 32,
                                                (2 * hp) * 128:(2 * hp + 1) * 128],
                                        rhs_a, start=True, stop=True,
                                        tile_position=tp))
                        pe.inc(t.matmul(psb[i % 2][:, :],
                                        vsel_sb[32 * gg:32 * gg + 32,
                                                (2 * hp + 1) * 128:(2 * hp + 2) * 128],
                                        rhs_b, start=True, stop=True,
                                        tile_position=tp))
                        i += 1

            @blk.scalar
            def _(sc):
                i = 0
                for to in range(NTO):
                    for ib in range(NIB):
                        sc.wait_ge(pe.h, pe0 + 2 * i + 1)
                        if i >= 4:
                            sc.wait_ge(osem[i % 2].h, oval[i - 4])
                        se.inc(sc.copy(wtb[i % 4][:, :], psa[i % 2][:, :]))
                        i += 1

            @blk.vector
            def _(v):
                i = 0
                for to in range(NTO):
                    for ib in range(NIB):
                        v.wait_ge(pe.h, pe0 + 2 * i + 2)
                        v.wait_ge(se.h, se0 + i + 1)
                        v.wait_ge(xsem[i % 4].h, xval[i])
                        ve.inc(v.copy_predicated(wtb[i % 4][:, :], msk[i % 4][:, :],
                                                 psb[i % 2][:, :]))
                        i += 1

            @blk.gpsimd
            def _(g):
                i = 0
                for to in range(NTO):
                    for ib in range(NIB):
                        g.wait_ge(ve.h, ve0 + i + 1)
                        if i >= 2:
                            g.wait_ge(osem[i % 2].h, oval[i - 2])
                        osem[i % 2].dinc(g.dma_start(
                            out=AP(wdst, (to * 128) * OUTC + ib * 512,
                                   [[OUTC, 128], [1, 512]]),
                            in_=wtb[i % 4][:, :]))
                        assert oval[i] == osem[i % 2].n
                        i += 1
                g.wait_ge(osem[0].h, osem[0].n)
                g.wait_ge(osem[1].h, osem[1].n)
        ssp.close()

    for widx in range(3):
        wstk = ExitStack()
        cols = INTER if widx < 2 else NQ * E
        F8cur["t"] = wstk.enter_context(nc.sbuf_tensor(_nm("F8"), [128, cols], BF16))
        p0d(widx, HOSTR)
        p0e(widx)
        wstk.close()
    spl.close()
    mrow_sb = es.enter_context(nc.sbuf_tensor(_nm("mrow_sb"), [1, TOK], F32))
    rrow_sb = es.enter_context(nc.sbuf_tensor(_nm("rrow_sb"), [1, TOK], F32))
    trow_sb = es.enter_context(nc.sbuf_tensor(_nm("trow_sb"), [1, TOK], F32))
    nxt_sb = [es.enter_context(nc.sbuf_tensor(_nm(f"nxt{i}"), [128, TCH], F32))
              for i in range(2)]
    nt1_sb = es.enter_context(nc.sbuf_tensor(_nm("nt1g"), [128, TCH], F32))
    PREVN = {"ve": 0}

    class _MockInstr:
        def then_inc(self, *a, **k):
            return self

    class _MockEng:
        def __getattr__(self, name):
            return lambda *a, **k: _MockInstr()

    class P:
        def __init__(self, sems_):
            self.evt = {}
            self.shadow = {}
            self.emit = False
            self.sems = sems_

        def start_record(self):
            self.emit = False
            self.evt = {}
            self.shadow = {k: s.n for k, s in self.sems.items()}

        def start_emit(self):
            self.emit = True
            self.shadow = {k: s.n for k, s in self.sems.items()}

        def inc(self, sname, key, instr=None, k=1):
            if not self.emit:
                self.shadow[sname] += k
                self.evt[key] = (sname, self.shadow[sname])
            else:
                s = self.sems[sname]
                instr.then_inc(s.h, k)
                s.n += k
                assert self.evt[key] == (sname, s.n), (key, self.evt[key], s.n)

        def dinc(self, sname, key, instr=None):
            self.inc(sname, key, instr, 16)

        def wait(self, eng, key):
            if self.emit and key in self.evt:
                sname, val = self.evt[key]
                eng.wait_ge(self.sems[sname].h, val)

    PH = P(sems)

    def run_phase(bodies):
        PH.start_record()
        for name, fn in bodies:
            fn(_MockEng())
        PH.start_emit()
        with nc.Block() as blk:
            for name, fn in bodies:
                getattr(blk, name)(fn)

    def phase_norm(xsrc, tch, xn_sb, oguard):
        ssp = ExitStack()
        psm_ = ssp.enter_context(nc.psum_tensor(_nm("ln_pm"), [128, TCH], F32))
        psr_ = ssp.enter_context(nc.psum_tensor(_nm("ln_pr"), [128, TCH], F32))
        xt = nxt_sb
        t1 = nt1_sb
        prev_ve = PREVN["ve"]

        def b_scalar(sc):
            if tch == 0:
                sc.mul(mrow_sb[0:1, :], sumx_sb[0:1, :], 1.0 / E)
                sc.drain()
                sc.square(trow_sb[0:1, :], mrow_sb[0:1, :])
                PH.inc("se", "lnr1", sc.mul(rrow_sb[0:1, :], sumx2_sb[0:1, :], 1.0 / E))
                PH.wait(sc, "lnrv")
                sc.activation(trow_sb[0:1, :], trow_sb[0:1, :], AF.Ln, bias=1e-5)
                sc.drain()
                PH.inc("se", "lnr2",
                       sc.activation(rrow_sb[0:1, :], trow_sb[0:1, :], AF.Exp, scale=-0.5))

        def b_vector(v):
            if tch == 0:
                PH.wait(v, "lnr1")
                PH.inc("ve", "lnrv", v.tensor_tensor(trow_sb[0:1, :], rrow_sb[0:1, :],
                                                     trow_sb[0:1, :], ALU.subtract))
            for f in range(F):
                PH.wait(v, "bcast")
                PH.wait(v, ("nx", f))
                if f > 0:
                    v.drain()
                v.tensor_tensor(t1[:, :], xt[f % 2][:, :], psm_[:, :], ALU.subtract)
                v.drain()
                PH.inc("ve", ("xn", f),
                       v.tensor_tensor(xn_sb[:, f * TCH:(f + 1) * TCH], t1[:, :],
                                       psr_[:, :], ALU.mult))

        def b_tensor(t):
            if tch == 0:
                PH.wait(t, "lnr2")
            for su in range(NSUB):
                t.matmul(psm_[:, su * SUB:(su + 1) * SUB], ones_f[0:1, 0:128],
                         mrow_sb[0:1, tch * TCH + su * SUB: tch * TCH + (su + 1) * SUB],
                         start=True, stop=True)
                mm = t.matmul(psr_[:, su * SUB:(su + 1) * SUB], ones_f[0:1, 0:128],
                              rrow_sb[0:1, tch * TCH + su * SUB: tch * TCH + (su + 1) * SUB],
                              start=True, stop=True)
            PH.inc("pe", "bcast", mm)

        def b_gp(g):
            if oguard[0] > 0:
                g.wait_ge(sems["o0"].h, oguard[0])
            if oguard[1] > 0:
                g.wait_ge(sems["o1"].h, oguard[1])
            if prev_ve > 0:
                g.wait_ge(sems["ve"].h, prev_ve)
            for f in range(F):
                if f >= 2:
                    PH.wait(g, ("xn", f - 2))
                PH.dinc("x0" if f % 2 == 0 else "x1", ("nx", f),
                        g.dma_start(out=xt[f % 2][:, :],
                                    in_=xsrc.ap()[f * 128:(f + 1) * 128,
                                                  tch * TCH:(tch + 1) * TCH]))

        run_phase([("scalar", b_scalar), ("vector", b_vector),
                   ("tensor", b_tensor), ("gpsimd", b_gp)])
        ssp.close()
        PREVN["ve"] = sems["ve"].n

    def phase_qkv(layer, tch, xn_sb):
        rope = (layer == 0)
        NOT = 3 * E // 128
        NOG = NOT // 4
        ssp = ExitStack()
        wbuf = [ssp.enter_context(nc.sbuf_tensor(_nm(f"qw{i}"), [128, F * 512], BF16))
                for i in range(2)]
        qb_bf = [ssp.enter_context(nc.sbuf_tensor(_nm(f"qb{i}"), [128, SUB], BF16))
                 for i in range(2)]
        ro_bf = [ssp.enter_context(nc.sbuf_tensor(_nm(f"ro{i}"), [128, SUB], BF16))
                 for i in range(2)]
        tt1 = ssp.enter_context(nc.sbuf_tensor(_nm("qt1"), [128, SUB], F32))
        tt2 = ssp.enter_context(nc.sbuf_tensor(_nm("qt2"), [128, SUB], F32))
        nsq = [ssp.enter_context(nc.sbuf_tensor(_nm(f"qs{i}"), [128, SUB], BF16))
               for i in range(2)]
        nrow = [ssp.enter_context(nc.sbuf_tensor(_nm(f"nr{i}"), [1, SUB], BF16))
                for i in range(2)]
        ps = [ssp.enter_context(nc.psum_tensor(_nm(f"qp{i}"), [128, SUB], F32))
              for i in range(2)]
        psro = [ssp.enter_context(nc.psum_tensor(_nm(f"qr{i}"), [128, SUB], F32))
                for i in range(2)]
        psn = [ssp.enter_context(nc.psum_tensor(_nm(f"qpn{i}"), [1, SUB], F32))
               for i in range(2)]
        plan = []
        for og in range(NOG):
            for ot4 in range(4):
                for ts in range(NSUB):
                    ot = og * 4 + ot4
                    plan.append(dict(og=og, ot4=ot4, ts=ts, w=ot // F, h=ot % F))

        def b_wload_one(eng, og):
            if og >= 2:
                PH.wait(eng, ("qk", (og - 1) * 4 * NSUB - 1))
            if layer == 0:
                src = AP(wqkvT, og * 512, [[3 * E, 128], [128 * 3 * E, F], [1, 512]])
            else:
                wsel = [wq, wk, wv][(og * 512) // E]
                src = AP(wsel, (og * 512) % E, [[E, 128], [128 * E, F], [1, 512]])
            names = ("w0", "w1") if layer == 0 else ("g0", "g1")
            PH.dinc(names[og % 2], ("wb", og),
                    eng.dma_start(out=AP(
                wbuf[og % 2], 0, [[F * 512, 128], [512, F], [1, 512]]), in_=src))

        def b_wload(eng):
            for og in range(NOG):
                b_wload_one(eng, og)

        def b_tensor(t):
            for i, it in enumerate(plan):
                og, ot4, ts, w, h = it["og"], it["ot4"], it["ts"], it["w"], it["h"]
                PH.wait(t, ("wb", og))
                PH.wait(t, ("qbf", i - 2))
                if rope:
                    PH.wait(t, ("ropet1", i - 2))
                for f in range(F):
                    mm = t.matmul(ps[i % 2][:, :],
                                  wbuf[og % 2][:, f * 512 + ot4 * 128: f * 512 + ot4 * 128 + 128],
                                  xn_sb[:, f * TCH + ts * SUB: f * TCH + (ts + 1) * SUB],
                                  start=(f == 0), stop=(f == F - 1))
                PH.inc("pe", ("qk", i), mm)
                if w < 2 and rope:
                    PH.wait(t, ("qbf", i))
                    PH.wait(t, ("ropet2", i - 2))
                    PH.inc("po", ("rot", i),
                           t.matmul(psro[i % 2][:, :], rmt_sb[:, :], qb_bf[i % 2][:, :],
                                    start=True, stop=True))
                if w < 2:
                    PH.wait(t, ("nsq", i))
                    PH.wait(t, ("nrmevac", i - 2))
                    PH.inc("po", ("nmm", i),
                           t.matmul(psn[i % 2][:, :], ones_b[:, 0:1], nsq[i % 2][:, :],
                                    start=True, stop=True))

        def b_scalar(sc):
            for i, it in enumerate(plan):
                w, h = it["w"], it["h"]
                PH.wait(sc, ("qk", i))
                PH.wait(sc, ("rot", i - 2))
                PH.wait(sc, ("qkvdma", i - 2))
                if layer == 1:
                    op = sc.add(qb_bf[i % 2][:, :], ps[i % 2][:, :],
                                bias_sb[:, w * F + h: w * F + h + 1])
                else:
                    op = sc.copy(qb_bf[i % 2][:, :], ps[i % 2][:, :])
                PH.inc("se", ("qbf", i), op)
                if w < 2:
                    if rope:
                        PH.wait(sc, ("ropeadd", i))
                        src_bf = ro_bf[i % 2]
                    else:
                        src_bf = qb_bf[i % 2]
                        sc.drain()
                    PH.inc("se", ("nsq", i), sc.square(nsq[i % 2][:, :], src_bf[:, :]))
                    PH.wait(sc, ("nmm", i))
                    PH.wait(sc, ("nrmdma", i - 2))
                    PH.inc("se", ("nrmevac", i),
                           sc.copy(nrow[i % 2][0:1, :], psn[i % 2][0:1, :]))

        def b_vector(v):
            if not rope:
                return
            for i, it in enumerate(plan):
                w = it["w"]
                if w >= 2:
                    continue
                ts = it["ts"]
                tsl = slice(tch * TCH + ts * SUB, tch * TCH + (ts + 1) * SUB)
                PH.wait(v, ("qbf", i))
                PH.inc("ve", ("ropet1", i),
                       v.tensor_tensor(tt1[:, :], ps[i % 2][:, :], cos_sb[:, tsl],
                                       ALU.mult))
                PH.wait(v, ("rot", i))
                PH.inc("ve", ("ropet2", i),
                       v.tensor_tensor(tt2[:, :], psro[i % 2][:, :], sin_sb[:, tsl],
                                       ALU.mult))
                PH.wait(v, ("qkvdma", i - 2))
                PH.wait(v, ("nsq", i - 2))
                v.drain()
                PH.inc("ve", ("ropeadd", i),
                       v.tensor_tensor(ro_bf[i % 2][:, :], tt1[:, :], tt2[:, :],
                                       ALU.add))

        def b_store_rng(sy, i0, i1):
            for i in range(i0, i1):
                it = plan[i]
                w, h, ts = it["w"], it["h"], it["ts"]
                tc0 = ts * SUB
                grow = tch * GROWS + h * 3 * D + w * D
                if w < 2 and rope:
                    PH.wait(sy, ("ropeadd", i))
                    src_bf = ro_bf[i % 2]
                else:
                    PH.wait(sy, ("qbf", i))
                    src_bf = qb_bf[i % 2]
                PH.dinc("s0" if i % 2 == 0 else "s1", ("qkvdma", i),
                        sy.dma_start(out=AP(gin, grow * TCH + tc0,
                                            [[TCH, 128], [1, SUB]]),
                                     in_=src_bf[:, :]))
                if w < 2:
                    PH.wait(sy, ("nrmevac", i))
                    PH.dinc("r0" if i % 2 == 0 else "r1", ("nrmdma", i),
                            sy.dma_start(out=AP(gin, (tch * GROWS + NORM0 +
                                                      h * 2 + w) * TCH + tc0,
                                                [[TCH, 1], [1, SUB]]),
                                         in_=nrow[i % 2][0:1, :]))

        def b_store_tail(sy):
            for sn in ["s0", "s1", "r0", "r1"]:
                sy.wait_ge(sems[sn].h, sems[sn].n)

        def b_gather(g):
            PH.wait(g, ("qkvdma", len(plan) - 1))
            PH.wait(g, ("qkvdma", len(plan) - 2))
            nrm_is = [i2 for i2, it2 in enumerate(plan) if it2["w"] < 2]
            PH.wait(g, ("nrmdma", nrm_is[-1]))
            PH.wait(g, ("nrmdma", nrm_is[-2]))
            instr = g.collective_compute(
                "AllGather", ALU.bypass,
                replica_groups=[list(range(NCORES))],
                ins=[AP(gin, tch * GROWS * TCH, [[1, GROWS * TCH]])],
                outs=[AP(G, tch * 8 * GROWS * TCH, [[1, 8 * GROWS * TCH]])])
            if PH.emit:
                instr.then_inc(cc.h)
                cc.n += 1

        if layer == 0:
            def b_sync_all(sy):
                per_og = 4 * NSUB
                for og in range(NOG):
                    b_wload_one(sy, og)
                    b_store_rng(sy, og * per_og, (og + 1) * per_og)
                b_store_tail(sy)
            bodies = [("tensor", b_tensor), ("scalar", b_scalar),
                      ("vector", b_vector), ("sync", b_sync_all),
                      ("gpsimd", b_gather)]
        else:
            def b_sync_all(sy):
                b_store_rng(sy, 0, len(plan))
                b_store_tail(sy)

            def b_gp_all(g):
                b_wload(g)
                b_gather(g)
            bodies = [("tensor", b_tensor), ("scalar", b_scalar),
                      ("vector", b_vector), ("sync", b_sync_all),
                      ("gpsimd", b_gp_all)]
        run_phase(bodies)
        ssp.close()

    def phase_gather(src_t, dst_t):
        with nc.Block() as blk:
            @blk.gpsimd
            def _(g):
                g.collective_compute(
                    "AllGather", ALU.bypass,
                    replica_groups=[list(range(NCORES))],
                    ins=[src_t.ap()[:, :]],
                    outs=[dst_t.ap()[:, :]],
                ).then_inc(cc.h)
                cc.n += 1
                g.wait_ge(cc.h, cc.n)

    def phase_stage_attn():
        with nc.Block() as blk:
            @blk.sync
            def _(sy):
                sy.wait_ge(cc.h, cc.n)
                for rr in range(4):
                    sd.dinc(sy.dma_start(
                        out=AP(GL, rr * NHG * 3 * D * TOK,
                               [[TCH, NT], [TOK, NHG * 3 * D], [1, TCH]]),
                        in_=AP(G, regs["g"] + rr * GROWS * TCH,
                               [[8 * GROWS * TCH, NT], [1, NHG * 3 * D * TCH]])))
                    sd.dinc(sy.dma_start(
                        out=AP(NL, rr * 2 * NHG * TOK,
                               [[TCH, NT], [TOK, 2 * NHG], [1, TCH]]),
                        in_=AP(G, regs["nq"] + rr * GROWS * TCH,
                               [[8 * GROWS * TCH, NT], [1, 2 * NHG * TCH]])))
                sy.wait_ge(sd.h, sd.n)

    def phase_stage_out():
        with nc.Block() as blk:
            @blk.sync
            def _(sy):
                sy.wait_ge(cc.h, cc.n)
                for hp2 in range(NHG):
                    sd.dinc(sy.dma_start(
                        out=AP(AOL, hp2 * 4 * D * TOK, [[TOK, 4 * D], [1, TOK]]),
                        in_=AP(GO, regs["go"] + hp2 * 8 * D * S,
                               [[S, 4 * D], [1, TOK]])))
                sy.wait_ge(sd.h, sd.n)

    def phase_attn(layer):
        sm = C["SM_SCALE"]
        NQUAD = S // 512
        NKQB = S // QB
        NVJ = QB // 128
        setst = ExitStack()
        mhat_all = setst.enter_context(nc.sbuf_tensor(_nm("at_mha"), [128, NHG], F32))
        sustk = ExitStack()
        nall = sustk.enter_context(nc.sbuf_tensor(_nm("at_nall"), [8 * NHG, TOK], BF16))
        rmax = sustk.enter_context(nc.sbuf_tensor(_nm("at_rmax"), [8 * NHG, 1], F32))
        nmt32 = sustk.enter_context(nc.sbuf_tensor(_nm("at_nmt"), [1, 8 * NHG], F32))
        sca8 = sustk.enter_context(nc.sbuf_tensor(_nm("at_sca8"), [1, 2 * NHG], F32))
        mrow4 = sustk.enter_context(nc.sbuf_tensor(_nm("at_mr4"), [1, NHG], F32))
        ps_su = sustk.enter_context(nc.psum_tensor(_nm("at_psu"), [128, 32], F32))

        def b_su_sync(sy):
            PH.dinc("kd", "nld", sy.dma_start(out=nall[:, :], in_=NL.ap()[:, :]))

        def b_su_vector(v):
            PH.wait(v, "nld")
            PH.inc("ve", "rmx", v.reduce_max(rmax[:, 0:1], nall[:, :],
                                             axis=mybir.AxisListType.X))
            PH.wait(v, "sntr")
            v.tensor_copy(nmt32[0:1, :], ps_su[0:1, 0:8 * NHG])
            v.drain()
            for h2 in range(NHG):
                for w2 in range(2):
                    op = v.reduce_max(sca8[0:1, h2 * 2 + w2:h2 * 2 + w2 + 1],
                                      sbap(nmt32, 8 * NHG, 0, h2 * 2 + w2,
                                           [1, [2 * NHG, 4]]),
                                      axis=mybir.AxisListType.X)
            PH.inc("ve", "s8", op)
            PH.wait(v, "sq8")
            for h2 in range(NHG):
                op = v.tensor_tensor(mrow4[0:1, h2:h2 + 1],
                                     sca8[0:1, 2 * h2:2 * h2 + 1],
                                     sca8[0:1, 2 * h2 + 1:2 * h2 + 2], ALU.mult)
            PH.inc("ve", "mr4", op)
            PH.wait(v, "mbc4")
            PH.inc("ve", "mhall", v.tensor_copy(mhat_all[:, :], ps_su[:, 0:NHG]))

        def b_su_tensor(t):
            PH.wait(t, "rmx")
            PH.inc("po", "sntr", t.transpose(ps_su[0:1, 0:8 * NHG], rmax[:, 0:1],
                                             ident_f[0:8 * NHG, 0:8 * NHG]))
            PH.wait(t, "mh4")
            PH.inc("po", "mbc4", t.matmul(ps_su[:, 0:NHG], ones_f[0:1, 0:128],
                                          mrow4[0:1, :], start=True, stop=True))

        def b_su_scalar(sc):
            PH.wait(sc, "s8")
            PH.inc("se", "sq8", sc.sqrt(sca8[0:1, :], sca8[0:1, :]))
            PH.wait(sc, "mr4")
            PH.inc("se", "mh4", sc.mul(mrow4[0:1, :], mrow4[0:1, :], -sm * 1.02))

        run_phase([("sync", b_su_sync), ("vector", b_su_vector),
                   ("tensor", b_su_tensor), ("scalar", b_su_scalar)])
        sustk.close()

        for hp in range(NHG):
            asp = ExitStack()
            q_sb = asp.enter_context(nc.sbuf_tensor(_nm("at_q"), [128, S], BF16))
            k_sb = asp.enter_context(nc.sbuf_tensor(_nm("at_k"), [128, S], BF16))
            vt_sb = asp.enter_context(nc.sbuf_tensor(_nm("at_vt"), [128, S], BF16))
            vtmp = asp.enter_context(nc.sbuf_tensor(_nm("at_vm"), [128, QB], BF16))
            p_bf = [asp.enter_context(nc.sbuf_tensor(_nm(f"at_p{i}"), [128, 512], BF16))
                    for i in range(4)]
            lrow = [asp.enter_context(nc.sbuf_tensor(_nm(f"at_lr{i}"), [1, 512], F32))
                    for i in range(2)]
            lacc = [asp.enter_context(nc.sbuf_tensor(_nm(f"at_la{i}"), [128, 512], F32))
                    for i in range(2)]
            lb_sb = [asp.enter_context(nc.sbuf_tensor(_nm(f"at_lb{i}"), [128, 512], F32))
                     for i in range(2)]
            ob_bf = [asp.enter_context(nc.sbuf_tensor(_nm(f"at_o{i}"), [128, 512], BF16))
                     for i in range(2)]
            ps_st = [asp.enter_context(nc.psum_tensor(_nm(f"at_ps{i}"), [128, 512], F32))
                     for i in range(2)]
            ps_oT = [asp.enter_context(nc.psum_tensor(_nm(f"at_po{i}"), [128, 512], F32))
                     for i in range(2)]
            ps_l = [asp.enter_context(nc.psum_tensor(_nm(f"at_pl{i}"), [1, 512], F32))
                    for i in range(2)]
            ps_lb = asp.enter_context(nc.psum_tensor(_nm("at_plb"), [128, 512], F32))
            ps_vtm = asp.enter_context(nc.psum_tensor(_nm("at_pv"), [128, 256], BF16))
            ps_vt = [ps_vtm[:, 0:128], ps_vtm[:, 128:256]]

            iters = []
            for qq in range(NQUAD):
                for j in range(4 * qq + 4):
                    iters.append(dict(qq=qq, j=j, first=(j == 0),
                                      last=(j == 4 * qq + 3), doff=j - 4 * qq))
            NI = len(iters)
            vt_sched = [[] for _ in range(NQUAD)]
            built = 0
            for qq in range(NQUAD):
                need = (4 * qq + 3) // NVJ + 1
                while built < need:
                    vt_sched[qq].append(built)
                    built += 1

            def b_sync(sy):
                for rr in range(4):
                    PH.dinc("kd", ("qld", rr), sy.dma_start(
                        out=q_sb[:, rr * TOK:(rr + 1) * TOK],
                        in_=AP(GL, ((rr * NHG + hp) * 3 * D) * TOK,
                               [[TOK, 128], [1, TOK]])))
                    PH.dinc("kd", ("kld", rr), sy.dma_start(
                        out=k_sb[:, rr * TOK:(rr + 1) * TOK],
                        in_=AP(GL, ((rr * NHG + hp) * 3 * D + D) * TOK,
                               [[TOK, 128], [1, TOK]])))

                def ob_store(qx):
                    PH.wait(sy, ("ob", qx))
                    PH.dinc("o0" if qx % 2 == 0 else "o1", ("obdma", qx),
                            sy.dma_start(
                        out=AP(goin, (hp * D) * S + qx * 512, [[S, 128], [1, 512]]),
                        in_=ob_bf[qx % 2][:, :]))
                def vld(m):
                    PH.wait(sy, ("vtt", m * NVJ - 1))
                    rr = (m * QB) // TOK
                    co = (m * QB) % TOK
                    PH.dinc("vd", ("vld", m), sy.dma_start(
                        out=vtmp[:, :],
                        in_=AP(GL, ((rr * NHG + hp) * 3 * D + 2 * D) * TOK + co,
                               [[TOK, 128], [1, QB]])))
                for m in vt_sched[0] + (vt_sched[1] if NQUAD > 1 else []):
                    vld(m)
                for qq in range(NQUAD):
                    if qq + 2 < NQUAD:
                        for m in vt_sched[qq + 2]:
                            vld(m)
                    ob_store(qq)
                for sn in ["o0", "o1"]:
                    sy.wait_ge(sems[sn].h, sems[sn].n)

            def b_tensor(t):
                PH.wait(t, ("kld", 3))

                def emit_lav(ii):
                    it2 = iters[ii]
                    qq2, j2 = it2["qq"], it2["j"]
                    PH.wait(t, ("exp", ii))
                    if it2["first"]:
                        PH.wait(t, ("lrd", qq2 - 2))
                        PH.wait(t, ("ob", qq2 - 2))
                    PH.wait(t, ("vte", j2))
                    PH.inc("pe", ("av", ii),
                           t.matmul(ps_oT[qq2 % 2][:, :],
                                    vt_sb[:, j2 * 128:(j2 + 1) * 128],
                                    p_bf[ii % 4][:, :],
                                    start=it2["first"], stop=it2["last"]))
                    if it2["last"]:
                        PH.wait(t, ("lac", ii))
                        PH.inc("sp", ("lmmq", qq2),
                               t.matmul(ps_l[qq2 % 2][0:1, :], ones_f[:, 0:1],
                                        lacc[qq2 % 2][:, :], start=True, stop=True))
                        PH.wait(t, ("lrd", qq2))
                        PH.wait(t, ("lbe", qq2 - 1))
                        PH.inc("po", ("lbc", qq2),
                               t.matmul(ps_lb[:, :], ones_f[0:1, 0:128],
                                        lrow[qq2 % 2][0:1, :],
                                        start=True, stop=True))

                for i, it in enumerate(iters):
                    qq, j = it["qq"], it["j"]
                    if it["first"]:
                        for m in vt_sched[qq]:
                            PH.wait(t, ("vld", m))
                            for vj in range(NVJ):
                                gvj = m * NVJ + vj
                                PH.wait(t, ("vte", gvj - 1))
                                PH.inc("pw", ("vtt", gvj),
                                       t.transpose(ps_vt[gvj % 2][:, :],
                                                   vtmp[:, vj * 128:(vj + 1) * 128],
                                                   ident_b[:, :]))
                    PH.wait(t, ("exp", i - 2))
                    diag = it["doff"] >= 0
                    mm = t.matmul(ps_st[i % 2][:, :], k_sb[:, j * 128:(j + 1) * 128],
                                  q_sb[:, qq * 512:(qq + 1) * 512],
                                  start=True, stop=not diag)
                    if diag:
                        mm = t.matmul(ps_st[i % 2][:, :], ident_b[:, :],
                                      dmt_sb[:, it["doff"] * 512:
                                             (it["doff"] + 1) * 512],
                                      start=False, stop=True)
                    PH.inc("pe", ("qkt", i), mm)
                    if i - 2 >= 0:
                        emit_lav(i - 2)
                emit_lav(NI - 2)
                emit_lav(NI - 1)

            def b_scalar(sc):
                def lbe_emit(ii):
                    it2 = iters[ii]
                    if not it2["last"]:
                        return
                    qq2 = it2["qq"]
                    PH.wait(sc, ("lbc", qq2))
                    PH.wait(sc, ("ob", qq2 - 2))
                    PH.inc("se", ("lbe", qq2),
                           sc.copy(lb_sb[qq2 % 2][:, :], ps_lb[:, :]))

                for i, it in enumerate(iters):
                    if it["first"]:
                        for m in vt_sched[it["qq"]]:
                            for vj in range(NVJ):
                                gvj = m * NVJ + vj
                                PH.wait(sc, ("vtt", gvj))
                                PH.inc("se", ("vte", gvj),
                                       sc.copy(vt_sb[:, gvj * 128:(gvj + 1) * 128],
                                               ps_vt[gvj % 2]))
                    PH.wait(sc, ("qkt", i))
                    PH.wait(sc, ("av", i - 4))
                    PH.wait(sc, ("lac", i - 4))
                    PH.inc("se", ("exp", i),
                           sc.activation(p_bf[i % 4][:, :], ps_st[i % 2][:, :], AF.Exp,
                                         bias=mhat_all[:, hp:hp + 1], scale=sm))
                    if i - 2 >= 0:
                        lbe_emit(i - 2)
                lbe_emit(NI - 2)
                lbe_emit(NI - 1)

            def b_vector(v):
                def qend_emit(ii):
                    it2 = iters[ii]
                    if not it2["last"]:
                        return
                    qq2 = it2["qq"]
                    PH.wait(v, ("lmmq", qq2))
                    PH.inc("vf", ("lrd", qq2),
                           v.reciprocal(lrow[qq2 % 2][0:1, :], ps_l[qq2 % 2][0:1, :]))
                    PH.wait(v, ("lbe", qq2))
                    PH.wait(v, ("av", ii))
                    PH.wait(v, ("obdma", qq2 - 2))
                    v.drain()
                    PH.inc("vf", ("ob", qq2),
                           v.tensor_tensor(ob_bf[qq2 % 2][:, :], ps_oT[qq2 % 2][:, :],
                                           lb_sb[qq2 % 2][:, :], ALU.mult))

                for i, it in enumerate(iters):
                    qq = it["qq"]
                    PH.wait(v, ("exp", i))
                    v.drain()
                    if it["first"]:
                        PH.wait(v, ("lmmq", qq - 2))
                        op = v.tensor_copy(lacc[qq % 2][:, :], p_bf[i % 4][:, :])
                    else:
                        op = v.tensor_tensor(lacc[qq % 2][:, :], lacc[qq % 2][:, :],
                                             p_bf[i % 4][:, :], ALU.add)
                    PH.inc("ve", ("lac", i), op)
                    if i - 2 >= 0:
                        qend_emit(i - 2)
                qend_emit(NI - 2)
                qend_emit(NI - 1)

            def b_gp(g):
                PH.wait(g, ("obdma", NQUAD - 1))
                PH.wait(g, ("obdma", NQUAD - 2))
                instr = g.collective_compute(
                    "AllGather", ALU.bypass,
                    replica_groups=[list(range(NCORES))],
                    ins=[AP(goin, hp * D * S, [[1, D * S]])],
                    outs=[AP(GO, hp * 8 * D * S, [[1, 8 * D * S]])])
                if PH.emit:
                    instr.then_inc(cc.h)
                    cc.n += 1

            run_phase([("sync", b_sync), ("tensor", b_tensor),
                       ("scalar", b_scalar), ("vector", b_vector),
                       ("gpsimd", b_gp)])
            asp.close()
        setst.close()


    def phase_out(layer, tch, xsrc, xdst, stats):
        NOG = E // 512
        ssp = ExitStack()
        wbuf = [ssp.enter_context(nc.sbuf_tensor(_nm(f"ow{i}"), [128, F * 512], BF16))
                for i in range(2)]
        ao_sb = ssp.enter_context(nc.sbuf_tensor(_nm("ao_sb"), [128, F * TCH], BF16))
        xold = [ssp.enter_context(nc.sbuf_tensor(_nm(f"oxo{i}"), [128, SUB], F32))
                for i in range(2)]
        xnew = [ssp.enter_context(nc.sbuf_tensor(_nm(f"oxn{i}"), [128, SUB], F32))
                for i in range(2)]
        xbf = [ssp.enter_context(nc.sbuf_tensor(_nm(f"oxb{i}"), [128, SUB], BF16))
               for i in range(2)]
        x2bf = [ssp.enter_context(nc.sbuf_tensor(_nm(f"ox2{i}"), [128, SUB], BF16))
                for i in range(2)]
        ps = [ssp.enter_context(nc.psum_tensor(_nm(f"op{i}"), [128, SUB], F32))
              for i in range(2)]
        psx = [ssp.enter_context(nc.psum_tensor(_nm(f"opx{i}"), [1, SUB], F32))
               for i in range(NSUB)]
        psx2 = [ssp.enter_context(nc.psum_tensor(_nm(f"opx2{i}"), [1, SUB], F32))
                for i in range(NSUB)]
        plan = []
        for og in range(NOG):
            for ot4 in range(4):
                for ts in range(NSUB):
                    plan.append(dict(og=og, ot=og * 4 + ot4, ot4=ot4, ts=ts))
        NI = len(plan)
        o_pre = (sems["o0"].n, sems["o1"].n)

        def b_loads_head(sy):
            for f in range(F):
                rowc = ((f % NHG) * 4 * D + (f // NHG) * D) * TOK
                PH.dinc("kd", ("ao", f), sy.dma_start(
                    out=ao_sb[:, f * TCH:(f + 1) * TCH],
                    in_=AP(AOL, rowc + tch * TCH, [[TOK, 128], [1, TCH]])))

        def b_loads_rng(sy, i0, i1):
            for i in range(i0, i1):
                it = plan[i]
                PH.wait(sy, ("res", i - 2))
                PH.dinc("x0" if i % 2 == 0 else "x1", ("xold", i), sy.dma_start(
                    out=xold[i % 2][:, :],
                    in_=xsrc.ap()[it["ot"] * 128:(it["ot"] + 1) * 128,
                                  tch * TCH + it["ts"] * SUB:
                                  tch * TCH + (it["ts"] + 1) * SUB]))
                PH.wait(sy, ("x2b", i) if stats else ("res", i))
                PH.dinc("o0" if i % 2 == 0 else "o1", ("xnew", i), sy.dma_start(
                    out=AP(xdst, (it["ot"] * 128) * TOK + tch * TCH + it["ts"] * SUB,
                           [[TOK, 128], [1, SUB]]),
                    in_=xnew[i % 2][:, :]))

        def b_wload_one(eng, og):
            if og >= 2:
                PH.wait(eng, ("pmm", (og - 1) * 4 * NSUB - 1))
            if layer == 0:
                src = AP(woutT, og * 512, [[E, 128], [128 * E, F], [1, 512]])
            else:
                src = AP(wo, og * 512, [[E, 128], [128 * E, F], [1, 512]])
            names = ("w0", "w1") if layer == 0 else ("g0", "g1")
            PH.dinc(names[og % 2], ("wb", og), eng.dma_start(out=AP(
                wbuf[og % 2], 0, [[F * 512, 128], [512, F], [1, 512]]), in_=src))

        def b_wload(eng):
            for og in range(NOG):
                b_wload_one(eng, og)

        def b_tensor(t):
            for i, it in enumerate(plan):
                og, ot4, ts = it["og"], it["ot4"], it["ts"]
                PH.wait(t, ("wb", og))
                PH.wait(t, ("ao", F - 1))
                PH.wait(t, ("res", i - 2))
                for f in range(F):
                    mm = t.matmul(ps[i % 2][:, :],
                                  wbuf[og % 2][:, f * 512 + ot4 * 128: f * 512 + ot4 * 128 + 128],
                                  ao_sb[:, f * TCH + ts * SUB: f * TCH + (ts + 1) * SUB],
                                  start=(f == 0), stop=(f == F - 1))
                PH.inc("pe", ("pmm", i), mm)
                if stats:
                    PH.wait(t, ("x2b", i))
                    mm1 = t.matmul(psx[ts][0:1, :], ones_b[:, 0:1], xbf[i % 2][:, :],
                                   start=(it["ot"] == 0), stop=(it["ot"] == 4 * NOG - 1))
                    mm2 = t.matmul(psx2[ts][0:1, :], ones_b[:, 0:1], x2bf[i % 2][:, :],
                                   start=(it["ot"] == 0), stop=(it["ot"] == 4 * NOG - 1))
                    PH.inc("po", ("smm", i), mm2)

        def b_vector(v):
            v.wait_ge(sems["o0"].h, o_pre[0])
            v.wait_ge(sems["o1"].h, o_pre[1])
            for i, it in enumerate(plan):
                PH.wait(v, ("pmm", i))
                PH.wait(v, ("xold", i))
                PH.wait(v, ("xnew", i - 2))
                if stats:
                    PH.wait(v, ("x2b", i - 2))
                PH.inc("ve", ("res", i),
                       v.tensor_tensor(xnew[i % 2][:, :], xold[i % 2][:, :],
                                       ps[i % 2][:, :], ALU.add))
            if stats:
                for ts in range(NSUB):
                    PH.wait(v, ("smm", NI - 1))
                    v.tensor_copy(sumx_sb[0:1, tch * TCH + ts * SUB:
                                          tch * TCH + (ts + 1) * SUB], psx[ts][0:1, :])
                    PH.inc("ve", ("sev", ts),
                           v.tensor_copy(sumx2_sb[0:1, tch * TCH + ts * SUB:
                                                  tch * TCH + (ts + 1) * SUB],
                                         psx2[ts][0:1, :]))

        def b_scalar(sc):
            if not stats:
                return
            for i, it in enumerate(plan):
                PH.wait(sc, ("res", i))
                PH.wait(sc, ("smm", i - 2))
                PH.inc("se", ("xbf", i), sc.copy(xbf[i % 2][:, :], xnew[i % 2][:, :]))
                PH.inc("se", ("x2b", i), sc.square(x2bf[i % 2][:, :], xnew[i % 2][:, :]))

        if layer == 0:
            def b_sync_all(sy):
                b_loads_head(sy)
                per_og = 4 * NSUB
                for og in range(NOG):
                    b_wload_one(sy, og)
                    b_loads_rng(sy, og * per_og, (og + 1) * per_og)
        else:
            def b_sync_all(sy):
                b_loads_head(sy)
                b_loads_rng(sy, 0, len(plan))
        if layer == 0:
            bodies = [("tensor", b_tensor), ("vector", b_vector),
                      ("scalar", b_scalar), ("sync", b_sync_all)]
        else:
            bodies = [("tensor", b_tensor), ("vector", b_vector),
                      ("scalar", b_scalar), ("sync", b_sync_all), ("gpsimd", b_wload)]
        run_phase(bodies)
        ssp.close()

    def phase_ffn_gu(tch, xn_sb, inter_sb):
        NITP = ITILES // 2
        ssp = ExitStack()
        wg = [ssp.enter_context(nc.sbuf_tensor(_nm(f"fg{i}"), [128, F * 256], BF16))
              for i in range(2)]
        wu = [ssp.enter_context(nc.sbuf_tensor(_nm(f"fu{i}"), [128, F * 256], BF16))
              for i in range(2)]
        g_bf = [ssp.enter_context(nc.sbuf_tensor(_nm(f"fgb{i}"), [128, SUB], BF16))
                for i in range(2)]
        u_bf = [ssp.enter_context(nc.sbuf_tensor(_nm(f"fub{i}"), [128, SUB], BF16))
                for i in range(2)]
        psg = [ssp.enter_context(nc.psum_tensor(_nm(f"fpg{i}"), [128, SUB], F32))
               for i in range(2)]
        psu = [ssp.enter_context(nc.psum_tensor(_nm(f"fpu{i}"), [128, SUB], F32))
               for i in range(2)]
        plan = []
        for itp in range(NITP):
            for itt in range(2):
                for ts in range(NSUB):
                    plan.append(dict(itp=itp, it=itp * 2 + itt, itt=itt, ts=ts))
        NI = len(plan)

        def b_wload(sy):
            for itp in range(NITP):
                if itp >= 2:
                    PH.wait(sy, ("umm", (itp - 1) * 2 * NSUB - 1))
                sn = "w0" if itp % 2 == 0 else "w1"
                PH.dinc(sn, ("wg", itp), sy.dma_start(
                    out=AP(wg[itp % 2], 0, [[F * 256, 128], [256, F], [1, 256]]),
                    in_=AP(gwT, itp * 256, [[INTER, 128], [128 * INTER, F], [1, 256]])))
                PH.dinc(sn, ("wu", itp), sy.dma_start(
                    out=AP(wu[itp % 2], 0, [[F * 256, 128], [256, F], [1, 256]]),
                    in_=AP(uwT, itp * 256, [[INTER, 128], [128 * INTER, F], [1, 256]])))

        def b_tensor(t):
            for i, it in enumerate(plan):
                itp, itt, ts = it["itp"], it["itt"], it["ts"]
                PH.wait(t, ("wu", itp))
                PH.wait(t, ("gbf", i - 2))
                for f in range(F):
                    mm = t.matmul(psg[i % 2][:, :],
                                  wg[itp % 2][:, f * 256 + itt * 128: f * 256 + itt * 128 + 128],
                                  xn_sb[:, f * TCH + ts * SUB: f * TCH + (ts + 1) * SUB],
                                  start=(f == 0), stop=(f == F - 1))
                PH.inc("pe", ("gmm", i), mm)
                PH.wait(t, ("ubf", i - 2))
                for f in range(F):
                    mm = t.matmul(psu[i % 2][:, :],
                                  wu[itp % 2][:, f * 256 + itt * 128: f * 256 + itt * 128 + 128],
                                  xn_sb[:, f * TCH + ts * SUB: f * TCH + (ts + 1) * SUB],
                                  start=(f == 0), stop=(f == F - 1))
                PH.inc("pe", ("umm", i), mm)

        def b_scalar(sc):
            for i, it in enumerate(plan):
                PH.wait(sc, ("gmm", i))
                PH.wait(sc, ("imul", i - 2))
                PH.inc("se", ("gbf", i),
                       sc.activation(g_bf[i % 2][:, :], psg[i % 2][:, :], AF.Silu))
                PH.wait(sc, ("umm", i))
                PH.inc("se", ("ubf", i), sc.copy(u_bf[i % 2][:, :], psu[i % 2][:, :]))

        def b_vector(v):
            for i, it in enumerate(plan):
                PH.wait(v, ("ubf", i))
                PH.inc("ve", ("imul", i),
                       v.tensor_tensor(inter_sb[:, it["it"] * TCH + it["ts"] * SUB:
                                                it["it"] * TCH + (it["ts"] + 1) * SUB],
                                       g_bf[i % 2][:, :], u_bf[i % 2][:, :], ALU.mult))

        run_phase([("sync", b_wload), ("tensor", b_tensor),
                   ("scalar", b_scalar), ("vector", b_vector)])
        ssp.close()

    def phase_ffn_down(tch, inter_sb, xsrc, xdst, stats, final):
        NOG2 = E // 256
        ssp = ExitStack()
        wdn = [ssp.enter_context(nc.sbuf_tensor(_nm(f"fd{i}"), [128, ITILES * 256], BF16))
               for i in range(2)]
        xold = [ssp.enter_context(nc.sbuf_tensor(_nm(f"dxo{i}"), [128, SUB], F32))
                for i in range(2)]
        xnew = [ssp.enter_context(nc.sbuf_tensor(_nm(f"dxn{i}"), [128, SUB], F32))
                for i in range(2)]
        xbf = [ssp.enter_context(nc.sbuf_tensor(_nm(f"dxb{i}"), [128, SUB], BF16))
               for i in range(2)]
        x2bf = [ssp.enter_context(nc.sbuf_tensor(_nm(f"dx2{i}"), [128, SUB], BF16))
                for i in range(2)]
        ps = [ssp.enter_context(nc.psum_tensor(_nm(f"dp{i}"), [128, SUB], F32))
              for i in range(2)]
        psx = [ssp.enter_context(nc.psum_tensor(_nm(f"dpx{i}"), [1, SUB], F32))
               for i in range(NSUB)]
        psx2 = [ssp.enter_context(nc.psum_tensor(_nm(f"dpx2{i}"), [1, SUB], F32))
                for i in range(NSUB)]
        plan = []
        for og in range(NOG2):
            for ot2 in range(2):
                for ts in range(NSUB):
                    plan.append(dict(og=og, ot=og * 2 + ot2, ot2=ot2, ts=ts))
        NI = len(plan)
        o_pre = (sems["o0"].n, sems["o1"].n)

        def b_wload_one(sy, og):
            if og >= 2:
                PH.wait(sy, ("dmm", (og - 1) * 2 * NSUB - 1))
            PH.dinc("w0" if og % 2 == 0 else "w1", ("wd", og), sy.dma_start(
                out=AP(wdn[og % 2], 0, [[ITILES * 256, 128], [256, ITILES], [1, 256]]),
                in_=AP(dwT, og * 256, [[E, 128], [128 * E, ITILES], [1, 256]])))

        def b_io_rng(sy, i0, i1):
            for i in range(i0, i1):
                it = plan[i]
                PH.wait(sy, ("res", i - 2))
                PH.dinc("x0" if i % 2 == 0 else "x1", ("xold", i), sy.dma_start(
                    out=xold[i % 2][:, :],
                    in_=xsrc.ap()[it["ot"] * 128:(it["ot"] + 1) * 128,
                                  tch * TCH + it["ts"] * SUB:
                                  tch * TCH + (it["ts"] + 1) * SUB]))
                if stats:
                    PH.wait(sy, ("x2b", i))
                else:
                    PH.wait(sy, ("res", i))
                PH.dinc("o0" if i % 2 == 0 else "o1", ("xnew", i), sy.dma_start(
                    out=AP(xdst, (it["ot"] * 128) * TOK + tch * TCH + it["ts"] * SUB,
                           [[TOK, 128], [1, SUB]]),
                    in_=xnew[i % 2][:, :]))

        def b_sync(sy):
            per_og = 2 * NSUB
            for og in range(NOG2):
                b_wload_one(sy, og)
                b_io_rng(sy, og * per_og, (og + 1) * per_og)
            for sn in ["o0", "o1"]:
                sy.wait_ge(sems[sn].h, sems[sn].n)

        def b_tensor(t):
            for i, it in enumerate(plan):
                og, ot2, ts = it["og"], it["ot2"], it["ts"]
                PH.wait(t, ("wd", og))
                PH.wait(t, ("res", i - 2))
                for fi in range(ITILES):
                    mm = t.matmul(ps[i % 2][:, :],
                                  wdn[og % 2][:, fi * 256 + ot2 * 128: fi * 256 + ot2 * 128 + 128],
                                  inter_sb[:, fi * TCH + ts * SUB: fi * TCH + (ts + 1) * SUB],
                                  start=(fi == 0), stop=(fi == ITILES - 1))
                PH.inc("pe", ("dmm", i), mm)
                if stats:
                    PH.wait(t, ("x2b", i))
                    t.matmul(psx[ts][0:1, :], ones_b[:, 0:1], xbf[i % 2][:, :],
                             start=(it["ot"] == 0), stop=(it["ot"] == 2 * NOG2 - 1))
                    mm2 = t.matmul(psx2[ts][0:1, :], ones_b[:, 0:1], x2bf[i % 2][:, :],
                                   start=(it["ot"] == 0), stop=(it["ot"] == 2 * NOG2 - 1))
                    PH.inc("po", ("smm", i), mm2)

        def b_vector(v):
            v.wait_ge(sems["o0"].h, o_pre[0])
            v.wait_ge(sems["o1"].h, o_pre[1])
            for i, it in enumerate(plan):
                PH.wait(v, ("dmm", i))
                PH.wait(v, ("xold", i))
                PH.wait(v, ("xnew", i - 2))
                if stats:
                    PH.wait(v, ("x2b", i - 2))
                PH.inc("ve", ("res", i),
                       v.tensor_tensor(xnew[i % 2][:, :], xold[i % 2][:, :],
                                       ps[i % 2][:, :], ALU.add))
            if stats:
                for ts in range(NSUB):
                    PH.wait(v, ("smm", NI - 1))
                    v.tensor_copy(sumx_sb[0:1, tch * TCH + ts * SUB:
                                          tch * TCH + (ts + 1) * SUB], psx[ts][0:1, :])
                    PH.inc("ve", ("sev", ts),
                           v.tensor_copy(sumx2_sb[0:1, tch * TCH + ts * SUB:
                                                  tch * TCH + (ts + 1) * SUB],
                                         psx2[ts][0:1, :]))

        def b_scalar(sc):
            if not stats:
                return
            for i, it in enumerate(plan):
                PH.wait(sc, ("res", i))
                PH.wait(sc, ("smm", i - 2))
                PH.inc("se", ("xbf", i), sc.copy(xbf[i % 2][:, :], xnew[i % 2][:, :]))
                PH.inc("se", ("x2b", i), sc.square(x2bf[i % 2][:, :], xnew[i % 2][:, :]))

        run_phase([("sync", b_sync), ("tensor", b_tensor),
                   ("vector", b_vector), ("scalar", b_scalar)])
        ssp.close()

    def sublayer_attn(layer, xsrc, xdst, stats):
        oguard = (sems["o0"].n, sems["o1"].n)
        asp = ExitStack()
        xn_sb = asp.enter_context(nc.sbuf_tensor(_nm("sx_xn"), [128, F * TCH], BF16))
        for tch in range(NT):
            phase_norm(xsrc, tch, xn_sb, oguard)
            phase_qkv(layer, tch, xn_sb)
        asp.close()
        phase_stage_attn()
        phase_attn(layer)
        phase_stage_out()
        asp = ExitStack()
        for tch in range(NT):
            phase_out(layer, tch, xsrc, xdst, stats)
        asp.close()

    def sublayer_ffn(xsrc, xdst, stats, final):
        oguard = (sems["o0"].n, sems["o1"].n)
        for tch in range(NT):
            asp = ExitStack()
            inter_sb = asp.enter_context(
                nc.sbuf_tensor(_nm("sf_in"), [128, ITILES * TCH], BF16))
            xstk = ExitStack()
            xn_sb = xstk.enter_context(nc.sbuf_tensor(_nm("sf_xn"), [128, F * TCH], BF16))
            phase_norm(xsrc, tch, xn_sb, oguard)
            phase_ffn_gu(tch, xn_sb, inter_sb)
            xstk.close()
            phase_ffn_down(tch, inter_sb, xsrc, xdst, stats,
                           final and tch == NT - 1)
            asp.close()

    sublayer_attn(0, xin, x1, stats=True)
    sublayer_ffn(x1, x2, stats=True, final=False)
    sublayer_attn(1, x2, x3, stats=True)
    sublayer_ffn(x3, xout, stats=False, final=True)
    if KDBG:
        with nc.Block() as blk:
            @blk.sync
            def _(sy):
                for dst, srct in [(dbg1, x1), (dbg2, x2), (dbg3, x3)]:
                    od.dinc(sy.dma_start(out=dst.ap()[:, :], in_=srct.ap()[:, :]))
                sy.wait_ge(od.h, od.n)

    es.close()
    nc.finalize()
    return nc



def host_static(C):
    g = spline_grid(C)
    mg, md = spline_masks(C, g)
    vsel = spline_vsel(C, g)
    Eg, Ed = spline_eperm(C)
    runs = mload_runs(C, g)
    return dict(grid=g, maskg=mg, maskd=md, vsel=vsel, Eg=Eg, Ed=Ed, runs=runs)


def make_inmaps(C, HS, x_full, wqkv, wout, wq, wk, wv, wo, bq, bk, bv, cps):
    E, F, TOK, D, H = C["E"], C["F"], C["TOK"], C["D"], C["H"]
    NHG, GROWS, NORM0, GOROWS, S = (C["NHG"], C["GROWS"], C["NORM0"],
                                    C["GOROWS"], C["S"])
    RT, NCTRL, TCH = C["RT"], C["NCTRL"], C["TCH"]
    g = HS["grid"]
    K1 = 128 // F
    hh = max(1, 128 // C["ITILES"])
    NQ = max(1, C["ITILES"] // 16)
    bias3 = np.zeros((128, 3 * F), dtype=np.float32)
    for w, b in enumerate([bq, bk, bv]):
        bias3[:, w * F:(w + 1) * F] = np.asarray(b, np.float32).reshape(F, 128).T
    cpads = np.stack([np.pad(np.asarray(c, np.float32), (2, 6), mode="edge")
                      for c in cps])
    phii = np.concatenate([g["phi_a"], g["phi_b"]], axis=1)
    dmaski = g["dmask"].astype(np.uint8)
    ident = np.eye(128, dtype=np.float32)
    ones = np.ones((128, 132), dtype=np.float32)
    ones[:, 130] = 1e-5
    ones[:, 131] = 0.0
    common = dict(
        wqkv=np.ascontiguousarray(np.asarray(wqkv, np.float32)),
        wout=np.ascontiguousarray(np.asarray(wout, np.float32)),
        wq=np.ascontiguousarray(np.asarray(wq, np.float32)),
        wk=np.ascontiguousarray(np.asarray(wk, np.float32)),
        wv=np.ascontiguousarray(np.asarray(wv, np.float32)),
        wo=np.ascontiguousarray(np.asarray(wo, np.float32)),
        biases=bias3, cpads=cpads,
        maskg=HS["maskg"].reshape(128, -1), maskd=HS["maskd"].reshape(128, -1),
        vseli=HS["vsel"],
        egi=HS["Eg"].reshape(K1 * 8 * 128, 128),
        edi=HS["Ed"].reshape(hh * NQ * 8 * 128, 128),
        identfi=ident, identbi=ident.astype(BF),
        onesi=ones, rmti=rot_matrix(),
        dmti=diag_masks(C), phii=phii, dmaski=dmaski,
    )
    in_maps = []
    for c in range(NCORES):
        b, qq = c // 4, c % 4
        pos = qq * TOK + np.arange(TOK)
        cosv, sinv = rope_tables(C, pos)
        coff = np.array([[(4 * b * GROWS + NHG * qq * 3 * D) * TCH,
                          (4 * b * GROWS + NORM0 + 2 * NHG * qq) * TCH,
                          4 * b * D * S + qq * TOK, 0]], dtype=np.uint32)
        xin = np.ascontiguousarray(
            np.asarray(x_full[b, qq * TOK:(qq + 1) * TOK, :], np.float32).T)
        m = dict(common)
        m.update(cosi=cosv, sini=sinv, coffi=coff, xin=xin)
        in_maps.append(m)
    return in_maps


_CACHE = {}


def kernel(L0_attn_a_x, L0_attn_a_Wqkv, L0_attn_a_Wout,
           L0_attn_b_Wq, L0_attn_b_bq, L0_attn_b_Wk, L0_attn_b_bk,
           L0_attn_b_Wv, L0_attn_b_bv, L0_attn_b_Wo,
           L0_ffn_gate_cp, L0_ffn_up_cp, L0_ffn_down_cp):
    C = make_cfg(True)
    if "nc" not in _CACHE:
        HS = host_static(C)
        _CACHE["HS"] = HS
        _CACHE["nc"] = build_program(C, HS)
    HS = _CACHE["HS"]
    nc = _CACHE["nc"]
    in_maps = make_inmaps(C, HS, np.asarray(L0_attn_a_x),
                          L0_attn_a_Wqkv, L0_attn_a_Wout,
                          L0_attn_b_Wq, L0_attn_b_Wk, L0_attn_b_Wv, L0_attn_b_Wo,
                          L0_attn_b_bq, L0_attn_b_bk, L0_attn_b_bv,
                          [L0_ffn_gate_cp, L0_ffn_up_cp, L0_ffn_down_cp])
    res = run_bass_kernel_spmd(nc, in_maps, core_ids=list(range(NCORES)))
    out = np.zeros((C["B"], C["S"], C["E"]), dtype=np.float32)
    TOK = C["TOK"]
    for c in range(NCORES):
        b, qq = c // 4, c % 4
        out[b, qq * TOK:(qq + 1) * TOK, :] = res.results[c]["xout"].T
    return out
